# version 08
# baseline (speedup 1.0000x reference)
import numpy as np
import ml_dtypes
import concourse.bass as bass
import concourse.mybir as mybir
from concourse.bass_utils import run_bass_kernel_spmd

F32 = mybir.dt.float32
BF16 = mybir.dt.bfloat16
AF = mybir.ActivationFunctionType
ALU = mybir.AluOpType
AX = mybir.AxisListType
BF = ml_dtypes.bfloat16

NCORE = 8
D = 1024
SEQ = 4096
NB = 4
TOK = 2048
NT = 16
NC = 8
MW = 8320
MC = 4096

ATTN_SCALE = {0: 1.0 / 1024.0, 1: 0.125, 2: 0.125, 3: 0.03125}



def _percol(v, p=128):
    v = np.asarray(v, dtype=np.float32)
    return np.ascontiguousarray(v.reshape(-1, p).T)


def _masktab(mv, shift):
    j = np.arange(MW)[None, :]
    p = np.arange(128)[:, None]
    d = j - MC - p + shift
    return mv(d).astype(BF)


def _mv1(d):
    w = (d >= 0) & (d <= 127)
    dil = (d >= 132) & ((d - 128) % 4 == 0) & (d <= 4095)
    return (w | dil).astype(np.float32)


def _mv2(d):
    return ((d >= 0) & (d <= 4095)).astype(np.float32)


def _mv0(d):
    return (d != 0).astype(np.float32)


def host_prep(inp):
    f32 = lambda a: np.ascontiguousarray(np.asarray(a, dtype=np.float32))
    shared = {}
    shared["c_ones_f"] = np.ones((128, 128), np.float32)
    shared["c_eye_f"] = np.eye(128, dtype=np.float32)
    bd = np.zeros((128, 2), np.float32)
    bd[0:64, 0] = 1.0
    bd[64:128, 1] = 1.0
    shared["c_bd64"] = bd
    shared["c_ones16"] = np.ones((128, 512), BF)
    shared["c_zeros16"] = np.zeros((128, 512), BF)
    eh = np.zeros((8, 8, 128), np.float32)
    for e in range(8):
        eh[e, e, :] = 1.0
    shared["c_eh"] = eh
    cst = np.zeros((128, 8), np.float32)
    cst[:, 0] = 1e-5
    cst[:, 2] = 1.0
    shared["c_cst"] = cst

    shared["w0_qT"] = f32(inp["L0_attn_Wq"].T)
    shared["w0_kT"] = f32(inp["L0_attn_Wk"].T)
    shared["w0_vT"] = f32(inp["L0_attn_Wv"].T)
    shared["w0_bq"] = _percol(inp["L0_attn_bq"])
    shared["w0_bk"] = _percol(inp["L0_attn_bk"])
    shared["w0_bv"] = _percol(inp["L0_attn_bv"])
    shared["w0_bvr"] = f32(inp["L0_attn_bv"]).reshape(1, 1024)
    shared["w0_w1"] = f32(inp["L0_ffn_w1"][0])
    shared["w0_w2"] = f32(inp["L0_ffn_w2"][0])
    shared["w0_b1"] = _percol(inp["L0_ffn_b1"][0])
    shared["w0_b2r"] = f32(inp["L0_ffn_b2"][0]).reshape(1, 1024)
    shared["w0_gam"] = _percol(inp["L0_ffn_gamma"])
    shared["w0_bet"] = _percol(inp["L0_ffn_beta"])
    shared["w1_qkvT"] = f32(inp["L1_attn_qkv"].T)
    shared["w1_woT"] = f32(inp["L1_attn_wo"].T)
    rm = np.asarray(inp["L1_ffn_rail_maker"], np.float32)
    rmbd = np.zeros((4, 8, 128, 128), np.float32)
    for h in range(4):
        for rg in range(8):
            for u in range(8):
                rmbd[h, rg, 16 * u:16 * u + 16, 16 * u:16 * u + 16] = rm[h, 8 * rg + u]
    shared["w1_rm"] = np.ascontiguousarray(rmbd.reshape(4096, 128))
    rup = np.asarray(inp["L1_ffn_rail_up_proj"], np.float32)
    rupbd = np.zeros((4, 8, 4, 128, 128), np.float32)
    for h in range(4):
        for rg in range(8):
            for i in range(4):
                for u in range(2):
                    r = 8 * rg + 2 * i + u
                    rupbd[h, rg, i, 16 * (2 * i + u):16 * (2 * i + u) + 16,
                          64 * u:64 * u + 64] = rup[h, r].T
    shared["w1_rup"] = np.ascontiguousarray(rupbd.reshape(16384, 128))
    cpT = f32(inp["L1_ffn_w_cproj"].T)
    shared["w1_cpT"] = np.ascontiguousarray(
        cpT.reshape(128, 128, 8, 128).transpose(2, 1, 0, 3).reshape(1024, 16384))
    wqkvT = f32(inp["L2_attn_w_qkv"].T)
    g = np.zeros_like(wqkvT)
    for h in range(16):
        g[:, 64 * h:64 * h + 64] = wqkvT[:, 192 * h:192 * h + 64]
        g[:, 1024 + 64 * h:1024 + 64 * h + 64] = wqkvT[:, 192 * h + 64:192 * h + 128]
        g[:, 2048 + 64 * h:2048 + 64 * h + 64] = wqkvT[:, 192 * h + 128:192 * h + 192]
    shared["w2_qkvT"] = g
    shared["w2_woT"] = f32(inp["L2_attn_w_proj"].T)
    shared["w2_bo"] = _percol(inp["L2_attn_b_proj"])
    shared["w2_cw1"] = f32(inp["L2_ffn_cn_w1"])
    shared["w2_cw2"] = f32(inp["L2_ffn_cn_w2"])
    shared["w2_cw3"] = f32(inp["L2_ffn_cn_w3"])
    shared["w2_cw4"] = f32(inp["L2_ffn_cn_w4"])
    shared["w2_cb1"] = _percol(inp["L2_ffn_cn_b1"])
    shared["w2_cb2"] = _percol(inp["L2_ffn_cn_b2"])
    shared["w2_cb3"] = _percol(inp["L2_ffn_cn_b3"])
    shared["w2_cb4"] = f32(inp["L2_ffn_cn_b4"]).reshape(1, 1)
    shared["w2_gw1"] = f32(inp["L2_ffn_gc_w1"])
    shared["w2_gw2"] = f32(inp["L2_ffn_gc_w2"])
    shared["w2_gw3"] = f32(inp["L2_ffn_gc_w3"])
    shared["w2_gb1"] = _percol(inp["L2_ffn_gc_b1"])
    shared["w2_gb2"] = _percol(inp["L2_ffn_gc_b2"])
    shared["w2_gb3"] = f32(inp["L2_ffn_gc_b3"]).reshape(1, 1)
    shared["w2_ar"] = f32(inp["L2_ffn_adaptive_range"]).reshape(1, 1)
    shared["w3_q"] = f32(inp["L3_attn_Wq"])
    shared["w3_k"] = f32(inp["L3_attn_Wk"])
    shared["w3_bq"] = _percol(inp["L3_attn_bq"])
    shared["w3_bk"] = _percol(inp["L3_attn_bk"])
    shared["w3_bkr"] = f32(inp["L3_attn_bk"]).reshape(1, 1024)
    rw = f32(inp["L3_ffn_router_w"])
    shared["w3_rw"] = np.ascontiguousarray(
        rw.reshape(8, 128, 8).transpose(1, 0, 2).reshape(128, 64))
    shared["w3_rbr"] = f32(inp["L3_ffn_router_b"]).reshape(1, 8)
    shared["w3_w1"] = f32(inp["L3_ffn_w1"]).reshape(8192, 4096)
    shared["w3_w2"] = f32(inp["L3_ffn_w2"]).reshape(32768, 1024)
    b1 = f32(inp["L3_ffn_b1"])
    shared["w3_b1"] = np.ascontiguousarray(
        b1.reshape(8, 32, 128).transpose(0, 2, 1).reshape(1024, 32))
    shared["w3_b2"] = f32(inp["L3_ffn_b2"])

    x0 = np.asarray(inp["L0_attn_x"], np.float32)
    percore = []
    for c in range(NCORE):
        b, par = c // 2, c % 2
        m = dict(shared)
        xt = x0[b].reshape(32, 128, D)[par::2].reshape(TOK, D)
        m["x_in"] = np.ascontiguousarray(xt.T)
        m["mt1_r0"] = _masktab(_mv1, 128 * (par - 0))
        m["mt1_r1"] = _masktab(_mv1, 128 * (par - 1))
        m["mt2_r0"] = _masktab(_mv2, 128 * (par - 0))
        m["mt2_r1"] = _masktab(_mv2, 128 * (par - 1))
        m["mt0_r0"] = _masktab(_mv0, 128 * (par - 0))
        m["mt0_r1"] = _masktab(_mv0, 128 * (par - 1))
        percore.append(m)
    return percore


def host_finish(outs):
    out = np.empty((NB, SEQ, D), np.float32)
    for c in range(NCORE):
        b, par = c // 2, c % 2
        out[b].reshape(32, 128, D)[par::2] = outs[c].T.reshape(NT, 128, D)
    return out


def q_g0(par, a):
    return (2 * a + par) * 128


def k_g0(r, kb):
    return (2 * (kb % 16) + r) * 128


def tile_live(l, qg, kb):
    if l in (0, 3):
        return True
    r = kb // 16
    for par in (0, 1):
        qmax = q_g0(par, 4 * qg + 3) + 127
        kmin = k_g0(r, kb)
        if qmax >= kmin:
            return True
    return False


def tile_needs_mask(l, qg, kb):
    if l == 1:
        return True
    r = kb // 16
    if l == 2:
        for par in (0, 1):
            qmin = q_g0(par, 4 * qg)
            kmax = k_g0(r, kb) + 127
            if qmin < kmax:
                return True
        return False
    if l == 0:
        if r != 0 and True:
            pass
        for par in (0, 1):
            for a in range(4):
                d0 = 256 * (4 * qg + a - (kb % 16)) + 128 * (par - r)
                if -511 <= d0 <= 127:
                    return True
        return False
    return False



class Ref:
    __slots__ = ("buf", "ap")

    def __init__(self, buf, ap):
        self.buf = buf
        self.ap = ap


class Buf:
    __slots__ = ("t", "name", "w", "r")

    def __init__(self, t, name):
        self.t = t
        self.name = name
        self.w = {}
        self.r = {}

    def __getitem__(self, idx):
        return Ref(self, self.t[idx])

    def ap(self, offset, dims):
        return Ref(self, bass.AP(self.t, offset, dims))

    @property
    def full(self):
        return Ref(self, self.t[:, :])


ENGS = ("T", "V", "S", "G", "Y")


NPOOL = 12


class Prog:
    def __init__(self, nc):
        self.nc = nc
        self.ops = {e: [] for e in ENGS}
        self.cnt = {"T": 0, "V": 0, "S": 0, "G": 0}
        self.qissue = {"y": 0, "g": 0}
        self.quse = {("y", i): 0 for i in range(NPOOL)}
        self.quse.update({("g", i): 0 for i in range(NPOOL)})
        self.cc = 0
        self.waited = {e: {} for e in ENGS}
        self._sb_stack = []

    def sbuf(self, name, shape, dtype):
        t = self.nc.alloc_sbuf_tensor(name, shape, dtype)
        return Buf(t, name)

    def psum(self, name, shape=(128, 512), dtype=F32):
        t = self.nc.alloc_psum_tensor(name, list(shape), dtype)
        return Buf(t, name)

    def dram(self, name, shape, dtype, kind="Internal"):
        t = self.nc.dram_tensor(name, list(shape), dtype, kind=kind)
        return Buf(t, name)

    def _wait(self, eng, key, val):
        if val <= 0:
            return
        if self.waited[eng].get(key, 0) >= val:
            return
        self.waited[eng][key] = val
        sems = None

        def w(e, key=key, val=val):
            e.wait_ge(self.sems[key], val)
        self.ops[eng].append(w)

    def _pre(self, eng, ins, outs):
        for ref in ins:
            for k, v in ref.buf.w.items():
                self._wait(eng, k, v)
        for ref in outs:
            for k, v in ref.buf.w.items():
                self._wait(eng, k, v)
            for k, v in ref.buf.r.items():
                self._wait(eng, k, v)

    def _post(self, key, cnt, ins, outs):
        for ref in outs:
            ref.buf.w = {key: cnt}
            ref.buf.r = {}
        for ref in ins:
            if ref.buf.r.get(key, 0) < cnt:
                ref.buf.r[key] = cnt

    def op(self, eng, ins, outs, fn):
        assert eng in ("T", "V", "S", "G")
        self._pre(eng, ins, outs)
        self.cnt[eng] += 1
        cnt = self.cnt[eng]

        def run(e, fn=fn, eng=eng, cnt=cnt):
            inst = fn(e)
            inst.then_inc(self.sems[eng], 1)
        self.ops[eng].append(run)
        self._post(eng, cnt, ins, outs)

    def mms(self, psum_ref, items, extra_ins=(), tile_position=None,
            start=True, stop=True):
        ins = [it[0] for it in items] + [it[1] for it in items] + list(extra_ins)
        self._pre("T", ins, [psum_ref])
        self.cnt["T"] += 1
        cnt = self.cnt["T"]
        n = len(items)

        def run(e, items=items, psum_ref=psum_ref, n=n, tp=tile_position,
                start=start, stop=stop):
            for i, (lh, rh) in enumerate(items):
                inst = e.matmul(psum_ref.ap, lh.ap, rh.ap,
                                start=(start and i == 0),
                                stop=(stop and i == n - 1),
                                tile_position=tp)
            inst.then_inc(self.sems["T"], 1)
        self.ops["T"].append(run)
        self._post("T", cnt, ins, [psum_ref])

    def transpose(self, psum_ref, in_ref, ident_ref):
        self.op("T", [in_ref, ident_ref], [psum_ref],
                lambda e: e.transpose(psum_ref.ap, in_ref.ap, ident_ref.ap))

    def dma(self, queue, out, in_, **kw):
        eng = "Y" if queue == "y" else "G"
        accum = kw.get("accum_op") is not None
        ins = [in_] + ([out] if accum else [])
        self._pre(eng, ins, [out])
        sidx = self.qissue[queue] % NPOOL
        self.qissue[queue] += 1
        key = (queue, sidx)
        self._wait(eng, key, self.quse[key])
        self.quse[key] += 16
        cnt = self.quse[key]

        kw = {k: v for k, v in kw.items() if v is not None}

        def run(e, out=out, in_=in_, kw=kw, key=key):
            e.dma_start(out=out.ap, in_=in_.ap, **kw).then_inc(self.sems[key], 16)
        self.ops[eng].append(run)
        self._post(key, cnt, [in_], [out])

    def coll(self, kind, op, groups, in_ref, out_ref):
        self._pre("G", [in_ref], [out_ref])
        self.cc += 1
        cnt = self.cc

        def run(e, kind=kind, op=op, groups=groups, in_ref=in_ref, out_ref=out_ref):
            e.collective_compute(kind, op, replica_groups=groups,
                                 ins=[in_ref.ap.opt()],
                                 outs=[out_ref.ap.opt()]).then_inc(self.sems["cc"], 1)
        self.ops["G"].append(run)
        self._post("cc", cnt, [in_ref], [out_ref])

    def barrier(self):
        tgts = {e: self.cnt[e] for e in ("T", "V", "S", "G")}
        tgts.update(self.quse)
        tgts["cc"] = self.cc
        for eng in ENGS:
            for k, v in tgts.items():
                self._wait(eng, k, v)

    def finish(self):
        self.barrier()

    def build_block(self):
        nc = self.nc
        import contextlib
        st = contextlib.ExitStack()
        self.sems = {}
        for name in ("T", "V", "S", "G", "cc"):
            self.sems[name] = st.enter_context(nc.semaphore("sem_" + name))
        for q in ("y", "g"):
            for i in range(NPOOL):
                self.sems[(q, i)] = st.enter_context(nc.semaphore(f"sem_{q}{i}"))
        blk = st.enter_context(nc.Block())
        ops = self.ops

        def mk(engname):
            def body(e):
                for f in ops[engname]:
                    f(e)
            return body
        blk.tensor(mk("T"))
        blk.vector(mk("V"))
        blk.scalar(mk("S"))
        blk.gpsimd(mk("G"))
        blk.sync(mk("Y"))
        st.close()



def v_copy(P, out, in_):
    P.op("V", [in_], [out], lambda e: e.tensor_copy(out.ap, in_.ap))


def v_tt(P, out, a, b, alu):
    P.op("V", [a, b], [out], lambda e: e.tensor_tensor(out.ap, a.ap, b.ap, alu))


def v_ts(P, out, in_, s1, op0, s2=None, op1=None, extra=()):
    ins = [in_] + [s for s in (s1, s2) if isinstance(s, Ref)] + list(extra)
    a1 = s1.ap if isinstance(s1, Ref) else s1
    a2 = s2.ap if isinstance(s2, Ref) else s2
    if op1 is None:
        P.op("V", ins, [out],
             lambda e: e.tensor_scalar(out.ap, in_.ap, a1, None, op0))
    else:
        P.op("V", ins, [out],
             lambda e: e.tensor_scalar(out.ap, in_.ap, a1, a2, op0, op1))


def v_stt(P, out, in0, scalar, in1, op0, op1):
    ins = [in0, in1] + ([scalar] if isinstance(scalar, Ref) else [])
    sc = scalar.ap if isinstance(scalar, Ref) else scalar
    P.op("V", ins, [out],
         lambda e: e.scalar_tensor_tensor(out.ap, in0.ap, sc, in1.ap, op0, op1))


def v_red(P, out, in_, alu=ALU.max, negate=False):
    P.op("V", [in_], [out],
         lambda e: e.tensor_reduce(out.ap, in_.ap, AX.X, alu, negate=negate))


def s_act(P, out, in_, func, bias=0.0, scale=1.0):
    ins = [in_] + ([bias] if isinstance(bias, Ref) else [])
    b = bias.ap if isinstance(bias, Ref) else bias
    P.op("S", ins, [out],
         lambda e: e.activation(out.ap, in_.ap, func, bias=b, scale=scale))


def g_memset(P, out, val):
    P.op("G", [], [out], lambda e: e.memset(out.ap, val))


def v_memset(P, out, val):
    P.op("V", [], [out], lambda e: e.memset(out.ap, val))



GROUPS = [[0, 1], [2, 3], [4, 5], [6, 7]]


class Ctx:
    pass


def _load_consts(P, C):
    nc = P.nc
    C.d_ones_f = P.dram("c_ones_f", [128, 128], F32, kind="ExternalInput")
    C.d_eye_f = P.dram("c_eye_f", [128, 128], F32, kind="ExternalInput")
    C.d_bd64 = P.dram("c_bd64", [128, 2], F32, kind="ExternalInput")
    C.d_ones16 = P.dram("c_ones16", [128, 512], BF16, kind="ExternalInput")
    C.d_zeros16 = P.dram("c_zeros16", [128, 512], BF16, kind="ExternalInput")
    C.d_eh = P.dram("c_eh", [8, 8, 128], F32, kind="ExternalInput")
    C.d_cst = P.dram("c_cst", [128, 8], F32, kind="ExternalInput")

    C.ones_f = P.sbuf("s_ones_f", [128, 128], F32)
    C.eye_f = P.sbuf("s_eye_f", [128, 128], F32)
    C.bd64 = P.sbuf("s_bd64", [128, 2], F32)
    C.ones16 = P.sbuf("s_ones16", [128, 512], BF16)
    C.cst = P.sbuf("s_cst", [128, 8], F32)
    P.dma("y", C.ones_f.full, C.d_ones_f.full)
    P.dma("y", C.eye_f.full, C.d_eye_f.full)
    P.dma("y", C.bd64.full, C.d_bd64.full)
    P.dma("y", C.ones16.full, C.d_ones16.full)
    P.dma("y", C.cst.full, C.d_cst.full)


def _bcast_row(P, C, out_f32, row_ref, pb, n):
    pref = Ref(pb, pb.t[0:128, 0:n])
    P.mms(pref, [(Ref(C.ones_f, C.ones_f.t[0:1, 0:128]), row_ref)])
    v_copy(P, out_f32, pref)


def emit_ln(P, C, l):
    st = P.nc.sbuf_tensor
    import contextlib
    ph = contextlib.ExitStack()
    xt = [Buf(ph.enter_context(st(f"ln{l}_x{i}", [128, 512], F32)), f"ln{l}_x{i}") for i in range(8)]
    sq = [Buf(ph.enter_context(st(f"ln{l}_sq{i}", [128, 512], F32)), f"ln{l}_sq{i}") for i in range(8)]
    t1 = Buf(ph.enter_context(st(f"ln{l}_t1", [128, 512], F32)), f"ln{l}_t1")
    t2 = Buf(ph.enter_context(st(f"ln{l}_t2", [128, 512], F32)), f"ln{l}_t2")
    bmu = Buf(ph.enter_context(st(f"ln{l}_bmu", [128, 512], F32)), f"ln{l}_bmu")
    brs = Buf(ph.enter_context(st(f"ln{l}_brs", [128, 512], F32)), f"ln{l}_brs")
    for tg in range(4):
        cols = slice(512 * tg, 512 * tg + 512)
        for c in range(8):
            P.dma("y", xt[c].full, C.x_src[c][:, cols])
        for c in range(8):
            s_act(P, sq[c].full, xt[c].full, AF.Square)
        pbx = C.pb[0] if tg % 2 == 0 else C.pb[2]
        pbq = C.pb[1] if tg % 2 == 0 else C.pb[3]
        P.mms(pbx.full, [(C.ones_f.full, xt[c].full) for c in range(8)])
        P.mms(pbq.full, [(C.ones_f.full, sq[c].full) for c in range(8)])
        v_ts(P, bmu.full, pbx.full, 1.0 / 1024.0, ALU.mult)
        P.dma("y", Ref(C.r_dram, C.r_dram.t[0:1, cols]), Ref(bmu, bmu.t[0:1, :]))
        v_ts(P, t1.full, pbq.full, 1.0 / 1024.0, ALU.mult)
        v_tt(P, t2.full, bmu.full, bmu.full, ALU.mult)
        v_tt(P, t1.full, t1.full, t2.full, ALU.subtract)
        s_act(P, t2.full, t1.full, AF.Ln, bias=Ref(C.cst, C.cst.t[:, 0:1]))
        s_act(P, brs.full, t2.full, AF.Exp, scale=-0.5)
        P.dma("y", Ref(C.r_dram, C.r_dram.t[1:2, cols]), Ref(brs, brs.t[0:1, :]))
        for c in range(8):
            v_tt(P, t2.full, xt[c].full, bmu.full, ALU.subtract)
            v_tt(P, Ref(C.xno[c], C.xno[c].t[:, cols]), t2.full, brs.full, ALU.mult)
            P.dma("y", Ref(C.gin, C.gin.t[128 * c:128 * c + 128, cols]),
                  Ref(C.xno[c], C.xno[c].t[:, cols]))
    P.barrier()
    ph.close()


def _xn32(P, C, dest, c, tg, bmu, brs, tmp):
    cols = slice(512 * tg, 512 * tg + 512)
    P.dma("y", tmp.full, C.x_src[c][:, cols])
    v_tt(P, dest, tmp.full, bmu.full, ALU.subtract)
    v_tt(P, dest, dest, brs.full, ALU.mult)


def _row_bcasts(P, C, bmu, brs, tg, pb=None):
    pb = pb if pb is not None else C.pb[5]
    cols = slice(512 * tg, 512 * tg + 512)
    P.dma("y", Ref(C.rowscr, C.rowscr.t[0:1, 0:512]), Ref(C.r_dram, C.r_dram.t[0:1, cols]))
    _bcast_row(P, C, bmu.full, Ref(C.rowscr, C.rowscr.t[0:1, 0:512]), pb, 512)
    P.dma("y", Ref(C.rowscr, C.rowscr.t[0:1, 0:512]), Ref(C.r_dram, C.r_dram.t[1:2, cols]))
    _bcast_row(P, C, brs.full, Ref(C.rowscr, C.rowscr.t[0:1, 0:512]), pb, 512)


class Chunk2:
    def __init__(self, buf, c0, w=128):
        self.buf = buf
        self.c0 = c0
        self.w = w

    @property
    def full(self):
        return Ref(self.buf, self.buf.t[:, self.c0:self.c0 + self.w])


class Chunk:
    def __init__(self, buf, r0, nrows=128):
        self.buf = buf
        self.r0 = r0
        self.nrows = nrows

    def __getitem__(self, idx):
        rows, cols = idx
        r = range(self.nrows)[rows]
        return Ref(self.buf, self.buf.t[self.r0 + r.start:self.r0 + r.stop, cols])


def emit_updates(P, C, l):
    k = 0
    for tg in range(4):
        cols = slice(512 * tg, 512 * tg + 512)
        for c in range(8):
            i = k % 3
            k += 1
            xt = Ref(C.u_x[i], C.u_x[i].t[:, :])
            st_ = Ref(C.u_s[i], C.u_s[i].t[:, :])
            gt = Ref(C.u_g[i], C.u_g[i].t[:, :])
            P.dma("y", xt, C.x_src[c][:, cols])
            P.dma("y", st_, Ref(C.s_dram, C.s_dram.t[128 * c:128 * c + 128, cols]))
            P.dma("y", gt, Ref(C.g_dram, C.g_dram.t[128 * c:128 * c + 128, cols]))
            if l == 0:
                v_tt(P, st_, st_, C.recip_l0[tg].full, ALU.mult)
            v_tt(P, st_, st_, gt, ALU.mult)
            v_tt(P, xt, xt, st_, ALU.add)
            P.dma("y", Ref(C.x_out, C.x_out.t[128 * c:128 * c + 128, cols]), xt)


def emit_gather(P, C):
    P.coll("AllGather", ALU.bypass, GROUPS, C.gin.full, C.gout.full)


def drain_interleaved(gens, head=None):
    prog = [0.0] * len(gens) if head is None else list(head)
    live = [True] * len(gens)
    while any(live):
        best, bv = -1, None
        for i, (g, w) in enumerate(gens):
            if live[i]:
                v = prog[i] / w
                if bv is None or v < bv:
                    best, bv = i, v
        try:
            next(gens[best][0])
            prog[best] += 1
        except StopIteration:
            live[best] = False


def _mkbuf(P, ph, name, shape, dtype):
    t = ph.enter_context(P.nc.sbuf_tensor(name, list(shape), dtype))
    return Buf(t, name)


def emit_attn_mh(P, C, l, ph):
    scale = ATTN_SCALE[l]
    has_proj = l in (1, 2)
    qkv_bias = (l == 3)
    v_from_k = (l == 3)

    kf = _mkbuf(P, ph, f"at{l}_kf", (128, 4096), BF16)
    vt = [_mkbuf(P, ph, f"at{l}_vt{i}", (128, 130), BF16) for i in range(32)]
    qf = _mkbuf(P, ph, f"at{l}_qf", (128, 2048), BF16)
    xn8 = [_mkbuf(P, ph, f"at{l}_xn8_{i}", (128, 512), BF16) for i in range(8)]
    et = [_mkbuf(P, ph, f"at{l}_et{i}", (128, 512), BF16) for i in range(2)]
    mk = [_mkbuf(P, ph, f"at{l}_mk{i}", (128, 512), BF16) for i in range(1)] * 2
    wqb = _mkbuf(P, ph, f"at{l}_wq", (128, 1024), BF16)
    wkb = _mkbuf(P, ph, f"at{l}_wk", (128, 1024), BF16)
    wvb = _mkbuf(P, ph, f"at{l}_wv", (128, 1024), BF16)
    wq = [Chunk2(wqb, 128 * i) for i in range(8)]
    wk = [Chunk2(wkb, 128 * i) for i in range(8)]
    wv = [Chunk2(wvb, 128 * i) for i in range(8)]
    mq4 = _mkbuf(P, ph, f"at{l}_mq4", (2, 16), F32)
    mred = _mkbuf(P, ph, f"at{l}_mred", (2, 4), F32)
    negrow = _mkbuf(P, ph, f"at{l}_negrow", (1, 2), F32)
    bm = _mkbuf(P, ph, f"at{l}_bm", (128, 2), F32)
    negln = _mkbuf(P, ph, f"at{l}_negln", (1, 512), F32)
    rb = _mkbuf(P, ph, f"at{l}_rb", (64, 512), F32)
    sst = _mkbuf(P, ph, f"at{l}_sst", (128, 512), F32)
    msq = sst
    s16 = _mkbuf(P, ph, f"at{l}_s16", (64, 512), BF16)
    bkb = _mkbuf(P, ph, f"at{l}_bkb", (128, 128), F32) if v_from_k else None
    wob = _mkbuf(P, ph, f"at{l}_wo", (128, 1024), BF16) if has_proj else None

    if qkv_bias:
        bq_sb = _mkbuf(P, ph, f"at{l}_bq", (128, 8), F32)
        bk_sb = _mkbuf(P, ph, f"at{l}_bk", (128, 8), F32)
        bkr_sb = _mkbuf(P, ph, f"at{l}_bkr", (1, 1024), F32)
        P.dma("y", bq_sb.full, C.W["w3_bq"].full)
        P.dma("y", bk_sb.full, C.W["w3_bk"].full)
        P.dma("y", bkr_sb.full, C.W["w3_bkr"].full)
    if l == 2:
        bo_sb = _mkbuf(P, ph, "at2_bo", (128, 8), F32)
        P.dma("y", bo_sb.full, C.W["w2_bo"].full)

    if l == 1:
        qkvT = C.W["w1_qkvT"]
    elif l == 2:
        qkvT = C.W["w2_qkvT"]

    for kt in range(32):
        v_memset(P, Ref(vt[kt], bass.AP(vt[kt].t, 64, [[130, 128], [65, 2], [1, 1]])), 1.0)

    live = {qg: [kb for kb in range(32) if tile_live(l, qg, kb)] for qg in range(4)}

    pb_bu = C.pb[0]
    pb_av = (C.pb[1], C.pb[2])
    pb_m3 = C.pb[3]
    pb_sr = (C.pb[0], C.pb[3])

    def _wload(dst, src, col0):
        W = src.t.shape[1]
        P.dma("g", Ref(dst, bass.AP(dst.t, 0, [[1024, 128], [128, 8], [1, 128]])),
              Ref(src, bass.AP(src.t, col0, [[W, 128], [W * 128, 8], [1, 128]])))

    for hp in range(8):
        if l == 3:
            _wload(wqb, C.W["w3_q"], 128 * hp)
            _wload(wkb, C.W["w3_k"], 128 * hp)
        else:
            _wload(wqb, qkvT, 128 * hp)
            _wload(wkb, qkvT, 1024 + 128 * hp)
            _wload(wvb, qkvT, 2048 + 128 * hp)
        wvv = wk if v_from_k else wv
        if v_from_k:
            _bcast_row(P, C, Ref(bkb, bkb.t[0:128, 0:128]),
                       Ref(bkr_sb, bkr_sb.t[0:1, 128 * hp:128 * hp + 128]), pb_m3, 128)
        for qtg in range(4):
            cols = slice(512 * qtg, 512 * qtg + 512)
            P.mms(pb_bu.full, [(wq[kc].full, Ref(C.xno[kc], C.xno[kc].t[:, cols])) for kc in range(8)])
            if qkv_bias:
                v_ts(P, Ref(qf, qf.t[:, cols]), pb_bu.full,
                     Ref(bq_sb, bq_sb.t[:, hp:hp + 1]), ALU.add)
            else:
                v_copy(P, Ref(qf, qf.t[:, cols]), pb_bu.full)
        for ktg in range(8):
            r, lc = ktg // 4, (ktg % 4) * 512
            cols = slice(512 * ktg, 512 * ktg + 512)
            for kc in range(8):
                P.dma("y", xn8[kc].full,
                      Ref(C.gout, C.gout.t[1024 * r + 128 * kc:1024 * r + 128 * kc + 128,
                                           lc:lc + 512]))
            P.mms(pb_bu.full, [(wk[kc].full, xn8[kc].full) for kc in range(8)])
            if qkv_bias:
                v_ts(P, Ref(kf, kf.t[:, cols]), pb_bu.full,
                     Ref(bk_sb, bk_sb.t[:, hp:hp + 1]), ALU.add)
            else:
                v_copy(P, Ref(kf, kf.t[:, cols]), pb_bu.full)
            for kt in range(4):
                kb = 4 * ktg + kt
                P.mms(Ref(pb_bu, pb_bu.t[:, 0:128]),
                      [(Ref(xn8[kc], xn8[kc].t[:, 128 * kt:128 * kt + 128]), wvv[kc].full)
                       for kc in range(8)])
                src = Ref(pb_bu, bass.AP(pb_bu.t, 0, [[512, 128], [64, 2], [1, 64]]))
                dst = Ref(vt[kb], bass.AP(vt[kb].t, 0, [[130, 128], [65, 2], [1, 64]]))
                if v_from_k:
                    bsrc = Ref(bkb, bass.AP(bkb.t, 0, [[128, 128], [64, 2], [1, 64]]))
                    v_stt(P, dst, src, 1.0, bsrc, ALU.mult, ALU.add)
                else:
                    v_copy(P, dst, src)
        for qtg in range(4):
            cols = slice(512 * qtg, 512 * qtg + 512)
            s_act(P, msq.full, Ref(qf, qf.t[:, cols]), AF.Square)
            P.mms(Ref(pb_m3, pb_m3.t[0:2, 0:512]), [(C.bd64.full, msq.full)])
            v_red(P, Ref(mq4, mq4.t[0:2, qtg:qtg + 1]), Ref(pb_m3, pb_m3.t[0:2, 0:512]))
        v_red(P, Ref(mred, mred.t[0:2, 0:1]), Ref(mq4, mq4.t[0:2, 0:4]))
        for ktg in range(8):
            cols = slice(512 * ktg, 512 * ktg + 512)
            s_act(P, msq.full, Ref(kf, kf.t[:, cols]), AF.Square)
            P.mms(Ref(pb_m3, pb_m3.t[0:2, 0:512]), [(C.bd64.full, msq.full)])
            v_red(P, Ref(mq4, mq4.t[0:2, 8 + ktg:9 + ktg]), Ref(pb_m3, pb_m3.t[0:2, 0:512]))
        v_red(P, Ref(mred, mred.t[0:2, 1:2]), Ref(mq4, mq4.t[0:2, 8:16]))
        v_tt(P, Ref(mred, mred.t[0:2, 2:3]), Ref(mred, mred.t[0:2, 0:1]),
             Ref(mred, mred.t[0:2, 1:2]), ALU.mult)
        s_act(P, Ref(mred, mred.t[0:2, 3:4]), Ref(mred, mred.t[0:2, 2:3]),
              AF.Sqrt, bias=Ref(C.cst, C.cst.t[0:2, 1:2]))
        v_ts(P, Ref(mred, mred.t[0:2, 0:1]), Ref(mred, mred.t[0:2, 3:4]),
             -scale, ALU.mult)
        P.transpose(Ref(pb_m3, pb_m3.t[0:1, 0:2]), Ref(mred, mred.t[0:2, 0:1]),
                    Ref(C.eye_f, C.eye_f.t[0:2, 0:2]))
        v_copy(P, negrow.full, Ref(pb_m3, pb_m3.t[0:1, 0:2]))
        for h in range(2):
            P.mms(Ref(pb_m3, pb_m3.t[0:128, 0:1]),
                  [(Ref(C.ones_f, C.ones_f.t[0:1, 0:128]),
                    Ref(negrow, negrow.t[0:1, h:h + 1]))])
            v_copy(P, Ref(bm, bm.t[:, h:h + 1]), Ref(pb_m3, pb_m3.t[0:128, 0:1]))

        for qg in range(4):
            qcols = slice(512 * qg, 512 * qg + 512)
            kbs = live[qg]
            steps = [(ikb, kb, h) for ikb, kb in enumerate(kbs) for h in range(2)]
            pend = None
            for si, (ikb, kb, h) in enumerate(steps):
                need_mask = tile_needs_mask(l, qg, kb)
                if need_mask and h == 0:
                    mkr = mk[ikb % 2]
                    tab = C.mtab[kb // 16]
                    O = MC + 256 * (4 * qg - (kb % 16))
                    P.dma("y", mkr.full,
                          Ref(tab, bass.AP(tab.t, O, [[MW, 128], [256, 4], [1, 128]])))
                etr = et[si % 2]
                psb = pb_sr[si % 2]
                P.mms(psb.full,
                      [(Ref(kf, kf.t[64 * h:64 * h + 64, 128 * kb:128 * kb + 128]),
                        Ref(qf, qf.t[64 * h:64 * h + 64, qcols]))])
                s_act(P, etr.full, psb.full, AF.Exp,
                      bias=Ref(bm, bm.t[:, h:h + 1]), scale=scale)
                if need_mask:
                    v_tt(P, etr.full, etr.full, mk[ikb % 2].full, ALU.mult)
                if pend is not None:
                    pikb, ph_, petr, pkb = pend
                    P.mms(Ref(pb_av[ph_], pb_av[ph_].t[0:65, :]),
                          [(Ref(vt[pkb], vt[pkb].t[:, 65 * ph_:65 * ph_ + 65]), petr)],
                          start=(pikb == 0), stop=(pikb == len(kbs) - 1))
                pend = (ikb, h, etr.full, kb)
            pikb, ph_, petr, pkb = pend
            P.mms(Ref(pb_av[ph_], pb_av[ph_].t[0:65, :]),
                  [(Ref(vt[pkb], vt[pkb].t[:, 65 * ph_:65 * ph_ + 65]), petr)],
                  start=(pikb == 0), stop=(pikb == len(kbs) - 1))
            for h in range(2):
                H = 2 * hp + h
                pav = pb_av[h]
                s_act(P, negln.full, Ref(pav, pav.t[64:65, :]), AF.Ln)
                P.mms(Ref(pb_m3, pb_m3.t[0:64, :]),
                      [(Ref(C.ones_f, C.ones_f.t[0:1, 0:64]), negln.full)])
                s_act(P, rb.full, Ref(pb_m3, pb_m3.t[0:64, :]), AF.Exp, scale=-1.0)
                if has_proj:
                    dst16 = Ref(s16, s16.t[0:64, :])
                    v_tt(P, dst16, Ref(pav, pav.t[0:64, :]), rb.full, ALU.mult)
                    P.dma("y", Ref(C.q_dram, C.q_dram.t[64 * H:64 * H + 64, qcols]), dst16)
                else:
                    dst = Ref(sst, sst.t[0:64, :])
                    v_tt(P, dst, Ref(pav, pav.t[0:64, :]), rb.full, ALU.mult)
                    P.dma("y", Ref(C.s_dram, C.s_dram.t[64 * H:64 * H + 64, qcols]), dst)
        yield

    if has_proj:
        woT = C.W["w1_woT"] if l == 1 else C.W["w2_woT"]
        wo = [Chunk2(wob, 128 * i) for i in range(8)]
        for mc in range(8):
            _wload(wob, woT, 128 * mc)
            for tg in range(4):
                cols = slice(512 * tg, 512 * tg + 512)
                for kc in range(8):
                    P.dma("y", xn8[kc].full,
                          Ref(C.q_dram, C.q_dram.t[128 * kc:128 * kc + 128, cols]))
                psb = pb_sr[tg % 2]
                P.mms(psb.full, [(wo[kc].full, xn8[kc].full) for kc in range(8)])
                if l == 2:
                    v_ts(P, sst.full, psb.full, Ref(bo_sb, bo_sb.t[:, mc:mc + 1]), ALU.add)
                else:
                    v_copy(P, sst.full, psb.full)
                P.dma("y", Ref(C.s_dram, C.s_dram.t[128 * mc:128 * mc + 128, cols]), sst.full)
            yield


def emit_gf(P, C, ph):
    gcs = _mkbuf(P, ph, "gf_gcs", (128, 10), F32)
    gc16 = _mkbuf(P, ph, "gf_gc16", (128, 8), BF16)
    gxt = _mkbuf(P, ph, "gf_gxt", (128, 1024), BF16)
    h1 = [_mkbuf(P, ph, f"gf_h1{i}", (128, 1), BF16) for i in range(4)]
    h2 = [_mkbuf(P, ph, f"gf_h2{i}", (128, 1), BF16) for i in range(2)]
    gw1 = [_mkbuf(P, ph, f"gf_w1{i}", (128, 512), BF16) for i in range(8)]
    gw2 = [_mkbuf(P, ph, f"gf_w2{i}", (128, 256), BF16) for i in range(4)]
    gw3 = [_mkbuf(P, ph, f"gf_w3{i}", (128, 1), BF16) for i in range(2)]
    gb1 = _mkbuf(P, ph, "gf_gb1", (128, 4), F32)
    gb2 = _mkbuf(P, ph, "gf_gb2", (128, 2), F32)
    gb3 = _mkbuf(P, ph, "gf_gb3", (1, 1), F32)
    for kc in range(8):
        P.dma("g", gw1[kc].full, Ref(C.W["w2_gw1"], C.W["w2_gw1"].t[128 * kc:128 * kc + 128, :]))
    for kc in range(4):
        P.dma("g", gw2[kc].full, Ref(C.W["w2_gw2"], C.W["w2_gw2"].t[128 * kc:128 * kc + 128, :]))
    for kc in range(2):
        P.dma("g", gw3[kc].full, Ref(C.W["w2_gw3"], C.W["w2_gw3"].t[128 * kc:128 * kc + 128, :]))
    P.dma("y", gb1.full, C.W["w2_gb1"].full)
    P.dma("y", gb2.full, C.W["w2_gb2"].full)
    P.dma("y", gb3.full, C.W["w2_gb3"].full)
    gcs4 = _mkbuf(P, ph, "gf_gcs4", (128, 4), F32)
    for c in range(8):
        for i in range(4):
            r, hh = i // 2, i % 2
            P.dma("y", gxt.full,
                  Ref(C.gout, C.gout.t[1024 * r + 128 * c:1024 * r + 128 * c + 128,
                                       1024 * hh:1024 * hh + 1024]))
            v_red(P, Ref(gcs4, gcs4.t[:, i:i + 1]), gxt.full, ALU.add)
        v_red(P, Ref(gcs, gcs.t[:, c:c + 1]), gcs4.full, ALU.add)
    v_copy(P, gc16.full, Ref(gcs, gcs.t[:, 0:8]))
    pb = C.pb[3]
    for mc in range(4):
        P.mms(Ref(pb, pb.t[0:128, 0:1]),
              [(Ref(gw1[kc], gw1[kc].t[:, 128 * mc:128 * mc + 128]),
                Ref(gc16, gc16.t[:, kc:kc + 1])) for kc in range(8)])
        s_act(P, h1[mc].full, Ref(pb, pb.t[0:128, 0:1]), AF.Gelu,
              bias=Ref(gb1, gb1.t[:, mc:mc + 1]), scale=1.0 / 4096.0)
    for mc in range(2):
        P.mms(Ref(pb, pb.t[0:128, 0:1]),
              [(Ref(gw2[kc], gw2[kc].t[:, 128 * mc:128 * mc + 128]), h1[kc].full)
               for kc in range(4)])
        s_act(P, h2[mc].full, Ref(pb, pb.t[0:128, 0:1]), AF.Gelu,
              bias=Ref(gb2, gb2.t[:, mc:mc + 1]))
    P.mms(Ref(pb, pb.t[0:1, 0:1]),
          [(Ref(gw3[kc], gw3[kc].t[:, 0:1]), h2[kc].full) for kc in range(2)])
    s_act(P, C.gf.full, Ref(pb, pb.t[0:1, 0:1]), AF.Sigmoid,
          bias=Ref(gb3, gb3.t[0:1, 0:1]))


def emit_attn_l0(P, C):
    import contextlib
    ph = contextlib.ExitStack()
    scale = ATTN_SCALE[0]
    XN = [_mkbuf(P, ph, f"a0_xn{c}", (128, 4096), BF16) for c in range(8)]
    qfc = [_mkbuf(P, ph, f"a0_q{c}", (128, 512), BF16) for c in range(8)]
    qst = [_mkbuf(P, ph, f"a0_qst{i}", (128, 512), BF16) for i in range(2)]
    kfc = [_mkbuf(P, ph, f"a0_k{c}", (128, 512), BF16) for c in range(8)]
    vtc = [_mkbuf(P, ph, f"a0_v{i}", (128, 1024), BF16) for i in range(4)]
    wk16 = [_mkbuf(P, ph, f"a0_wk{i}", (128, 1024), BF16) for i in range(8)]
    wv16 = [_mkbuf(P, ph, f"a0_wvf{i}", (128, 1024), BF16) for i in range(8)]
    wq16 = wk16
    et = [_mkbuf(P, ph, f"a0_et{i}", (128, 512), BF16) for i in range(4)]
    mk = _mkbuf(P, ph, "a0_mk", (128, 512), BF16)
    bq_sb = _mkbuf(P, ph, "a0_bq", (128, 8), F32)
    bk_sb = _mkbuf(P, ph, "a0_bk", (128, 8), F32)
    bvr_sb = _mkbuf(P, ph, "a0_bvr", (1, 1024), F32)
    mrow = _mkbuf(P, ph, "a0_mrow", (1, 32), F32)
    bmcol = _mkbuf(P, ph, "a0_bm", (128, 1), F32)
    den1 = _mkbuf(P, ph, "a0_den1", (1, 512), F32)
    negln = _mkbuf(P, ph, "a0_negln", (1, 512), F32)
    av2 = [_mkbuf(P, ph, f"a0_av{i}", (128, 512), F32) for i in range(2)]
    bvb = _mkbuf(P, ph, "a0_bvb", (128, 512), F32)
    wsq = _mkbuf(P, ph, "a0_wsq", (128, 512), F32)

    P.dma("y", bq_sb.full, C.W["w0_bq"].full)
    P.dma("y", bk_sb.full, C.W["w0_bk"].full)
    P.dma("y", bvr_sb.full, C.W["w0_bvr"].full)
    for c in range(8):
        P.dma("y", Ref(XN[c], XN[c].t[:, 0:2048]),
              Ref(C.gout, C.gout.t[128 * c:128 * c + 128, :]))
        P.dma("y", Ref(XN[c], XN[c].t[:, 2048:4096]),
              Ref(C.gout, C.gout.t[1024 + 128 * c:1024 + 128 * c + 128, :]))

    pb_s, pb_a0, pb_a1, pb_a2, pb_a3, pb_d, pb_x, pb_b = C.pb
    pb_a = [pb_a0, pb_a1, pb_a2, pb_a3]
    onescol_f = Ref(C.ones_f, C.ones_f.t[:, 0:1])
    onesrow_f = Ref(C.ones_f, C.ones_f.t[0:1, 0:128])
    ones16col = Ref(C.ones16, C.ones16.t[:, 0:1])

    for kc in range(8):
        P.dma("g", wq16[kc].full, Ref(C.W["w0_qT"], C.W["w0_qT"].t[128 * kc:128 * kc + 128, :]))
        P.dma("g", wv16[kc].full, Ref(C.W["w0_vT"], C.W["w0_vT"].t[128 * kc:128 * kc + 128, :]))
    for qtg in range(4):
        cols = slice(512 * qtg, 512 * qtg + 512)
        for mc in range(8):
            P.mms(pb_x.full, [(Ref(wq16[kc], wq16[kc].t[:, 128 * mc:128 * mc + 128]),
                               Ref(C.xno[kc], C.xno[kc].t[:, cols])) for kc in range(8)])
            qs = qst[mc % 2]
            v_ts(P, qs.full, pb_x.full, Ref(bq_sb, bq_sb.t[:, mc:mc + 1]), ALU.add)
            P.dma("y", Ref(C.q_dram, C.q_dram.t[128 * mc:128 * mc + 128, cols]), qs.full)
            s_act(P, wsq.full, qs.full, AF.Square)
            P.mms(Ref(pb_d, pb_d.t[0:1, 0:512]), [(onescol_f, wsq.full)],
                  start=(mc == 0), stop=(mc == 7))
        v_red(P, Ref(mrow, mrow.t[0:1, qtg:qtg + 1]), Ref(pb_d, pb_d.t[0:1, 0:512]))
    v_red(P, Ref(mrow, mrow.t[0:1, 8:9]), Ref(mrow, mrow.t[0:1, 0:4]))
    for ktg in range(8):
        cols = slice(512 * ktg, 512 * ktg + 512)
        for c in range(8):
            s_act(P, wsq.full, Ref(XN[c], XN[c].t[:, cols]), AF.Square)
            P.mms(Ref(pb_d, pb_d.t[0:1, 0:512]), [(onescol_f, wsq.full)],
                  start=(c == 0), stop=(c == 7))
        v_red(P, Ref(mrow, mrow.t[0:1, 16 + ktg:17 + ktg]), Ref(pb_d, pb_d.t[0:1, 0:512]))
    v_red(P, Ref(mrow, mrow.t[0:1, 9:10]), Ref(mrow, mrow.t[0:1, 16:24]))
    for kc in range(8):
        P.dma("g", wk16[kc].full, Ref(C.W["w0_kT"], C.W["w0_kT"].t[128 * kc:128 * kc + 128, :]))
    for kc in range(8):
        for hv in range(2):
            s_act(P, Ref(wsq, wsq.t[:, 0:512]),
                  Ref(wk16[kc], wk16[kc].t[:, 512 * hv:512 * hv + 512]), AF.Square)
            P.mms(Ref(pb_d, pb_d.t[0:1, 0:512]), [(onescol_f, Ref(wsq, wsq.t[:, 0:512]))],
                  start=(kc == 0 and hv == 0), stop=(kc == 7 and hv == 1))
    v_red(P, Ref(mrow, mrow.t[0:1, 10:11]), Ref(pb_d, pb_d.t[0:1, 0:512]), alu=ALU.add)
    s_act(P, Ref(wsq, wsq.t[0:128, 0:8]), bk_sb.full, AF.Square)
    P.mms(Ref(pb_d, pb_d.t[0:1, 0:8]), [(onescol_f, Ref(wsq, wsq.t[:, 0:8]))])
    v_red(P, Ref(mrow, mrow.t[0:1, 11:12]), Ref(pb_d, pb_d.t[0:1, 0:8]), alu=ALU.add)
    z2 = Ref(C.cst, C.cst.t[0:1, 1:2])
    v_tt(P, Ref(mrow, mrow.t[0:1, 12:13]), Ref(mrow, mrow.t[0:1, 10:11]),
         Ref(mrow, mrow.t[0:1, 9:10]), ALU.mult)
    s_act(P, Ref(mrow, mrow.t[0:1, 13:14]), Ref(mrow, mrow.t[0:1, 12:13]), AF.Sqrt, bias=z2)
    s_act(P, Ref(mrow, mrow.t[0:1, 14:15]), Ref(mrow, mrow.t[0:1, 11:12]), AF.Sqrt, bias=z2)
    v_tt(P, Ref(mrow, mrow.t[0:1, 13:14]), Ref(mrow, mrow.t[0:1, 13:14]),
         Ref(mrow, mrow.t[0:1, 14:15]), ALU.add)
    s_act(P, Ref(mrow, mrow.t[0:1, 15:16]), Ref(mrow, mrow.t[0:1, 8:9]), AF.Sqrt, bias=z2)
    v_tt(P, Ref(mrow, mrow.t[0:1, 15:16]), Ref(mrow, mrow.t[0:1, 15:16]),
         Ref(mrow, mrow.t[0:1, 13:14]), ALU.mult)
    v_ts(P, Ref(mrow, mrow.t[0:1, 15:16]), Ref(mrow, mrow.t[0:1, 15:16]), -scale, ALU.mult)
    P.mms(Ref(pb_b, pb_b.t[0:128, 0:1]), [(onesrow_f, Ref(mrow, mrow.t[0:1, 15:16]))])
    v_copy(P, bmcol.full, Ref(pb_b, pb_b.t[0:128, 0:1]))

    for kq in range(8):
        cols = slice(512 * kq, 512 * kq + 512)
        for mc in range(8):
            P.mms(pb_x.full, [(Ref(wk16[kc], wk16[kc].t[:, 128 * mc:128 * mc + 128]),
                               Ref(XN[kc], XN[kc].t[:, cols])) for kc in range(8)])
            v_ts(P, kfc[mc].full, pb_x.full,
                 Ref(bk_sb, bk_sb.t[:, mc:mc + 1]), ALU.add)
        for vc in range(2):
            P.mms(Ref(pb_b, pb_b.t[0:128, 0:512]),
                  [(onesrow_f, Ref(bvr_sb, bvr_sb.t[0:1, 512 * vc:512 * vc + 512]))])
            v_copy(P, bvb.full, Ref(pb_b, pb_b.t[0:128, 0:512]))
            for kt in range(4):
                kcols = slice(512 * kq + 128 * kt, 512 * kq + 128 * kt + 128)
                P.mms(Ref(pb_x, pb_x.t[:, 0:512]),
                      [(Ref(XN[kc], XN[kc].t[:, kcols]),
                        Ref(wv16[kc], wv16[kc].t[:, 512 * vc:512 * vc + 512])) for kc in range(8)])
                v_stt(P, Ref(vtc[kt], vtc[kt].t[:, 512 * vc:512 * vc + 512]),
                      Ref(pb_x, pb_x.t[:, 0:512]), 1.0, bvb.full, ALU.mult, ALU.add)
        for qg in range(4):
            qcols = slice(512 * qg, 512 * qg + 512)
            for kc in range(8):
                P.dma("y", qfc[kc].full,
                      Ref(C.q_dram, C.q_dram.t[128 * kc:128 * kc + 128, qcols]))
            pend = None
            for kt in range(4):
                kb = 4 * kq + kt
                psb = pb_s if kt % 2 == 0 else pb_x
                P.mms(psb.full,
                      [(Ref(kfc[kc], kfc[kc].t[:, 128 * kt:128 * kt + 128]),
                        qfc[kc].full) for kc in range(8)])
                s_act(P, et[kt].full, psb.full, AF.Exp, bias=bmcol.full, scale=scale)
                if tile_needs_mask(0, qg, kb):
                    tab = C.mtab[kb // 16]
                    O = MC + 256 * (4 * qg - (kb % 16))
                    P.dma("y", mk.full,
                          Ref(tab, bass.AP(tab.t, O, [[MW, 128], [256, 4], [1, 128]])))
                    v_tt(P, et[kt].full, et[kt].full, mk.full, ALU.mult)
                if pend is not None:
                    P.mms(Ref(pb_d, pb_d.t[0:1, 0:512]), [(ones16col, et[pend].full)],
                          start=(pend == 0), stop=False)
                pend = kt
            P.mms(Ref(pb_d, pb_d.t[0:1, 0:512]), [(ones16col, et[pend].full)],
                  start=(pend == 0), stop=True)
            v_copy(P, den1.full, Ref(pb_d, pb_d.t[0:1, 0:512]))
            P.dma("g", Ref(C.s_dram, C.s_dram.t[1024 + qg:1025 + qg, qcols]), den1.full,
                  accum_op=(None if kq == 0 else ALU.add))
            for dcp in range(2):
                for mcl in range(4):
                    mc = 4 * dcp + mcl
                    P.mms(Ref(pb_a[mcl], pb_a[mcl].t[:, 0:512]),
                          [(Ref(vtc[kt], vtc[kt].t[:, 128 * mc:128 * mc + 128]), et[kt].full)
                           for kt in range(4)])
                    r = av2[mcl % 2]
                    v_copy(P, r.full, Ref(pb_a[mcl], pb_a[mcl].t[:, 0:512]))
                    P.dma("g", Ref(C.s_dram, C.s_dram.t[128 * mc:128 * mc + 128, qcols]),
                          r.full, accum_op=(None if kq == 0 else ALU.add))
    for qg in range(4):
        qcols = slice(512 * qg, 512 * qg + 512)
        P.dma("y", den1.full, Ref(C.s_dram, C.s_dram.t[1024 + qg:1025 + qg, qcols]))
        s_act(P, negln.full, den1.full, AF.Ln)
        P.mms(Ref(pb_b, pb_b.t[0:128, 0:512]), [(onesrow_f, negln.full)])
        s_act(P, C.recip_l0[qg].full, Ref(pb_b, pb_b.t[0:128, 0:512]), AF.Exp, scale=-1.0)
    P.barrier()
    ph.close()


def emit_moe(P, C, l, ph):
    experts = [0] if l == 0 else list(range(8))
    nhalf = 1 if l == 0 else 2
    accw = 2048 // nhalf
    w1d = C.W["w0_w1"] if l == 0 else C.W["w3_w1"]
    w2d = C.W["w0_w2"] if l == 0 else C.W["w3_w2"]

    nset = 2 if l == 3 else 1
    w1t = [[_mkbuf(P, ph, f"mo{l}_w1_{b}_{i}", (128, 1024), BF16) for i in range(8)]
           for b in range(nset)]
    w2t = [[_mkbuf(P, ph, f"mo{l}_w2_0_{i}", (128, 1024), BF16) for i in range(8)]]
    ht = [_mkbuf(P, ph, f"mo{l}_h{i}", (128, 512), BF16) for i in range(8)]
    acc = [_mkbuf(P, ph, f"mo{l}_acc{i}", (128, accw), F32) for i in range(8)]
    b1_sb = _mkbuf(P, ph, f"mo{l}_b1", (128, 32), F32)
    g16 = [_mkbuf(P, ph, f"mo{l}_g16_0", (128, 512), BF16)] * 2
    pb_hr = (C.pb[4], C.pb[7])
    pb_or = (C.pb[5], C.pb[6])
    hctr = [0]
    octr = [0]

    if l == 0:
        b2r16 = _mkbuf(P, ph, "mo0_b2r", (1, 1024), BF16)
        zeros16 = _mkbuf(P, ph, "mo0_z16", (128, 512), BF16)
        P.dma("y", zeros16.full, C.d_zeros16.full)
        P.dma("g", b2r16.full, C.W["w0_b2r"].full)
        P.dma("y", b1_sb.full, C.W["w0_b1"].full)
    else:
        b216 = _mkbuf(P, ph, "mo3_b2", (8, 1024), BF16)
        P.dma("g", b216.full, C.W["w3_b2"].full)
        eh = _mkbuf(P, ph, "mo3_eh", (8, 128), F32)
        G = _mkbuf(P, ph, "mo3_G", (8, 2048), F32)
        G16 = _mkbuf(P, ph, "mo3_G16", (8, 2048), BF16)
        gb = [_mkbuf(P, ph, "mo3_gb0", (128, 512), F32)] * 2
        _emit_gates(P, C, ph, G, G16)

    for half in range(nhalf):
        tgs = list(range(4 // nhalf * half * nhalf // 1, 0)) if False else \
            list(range(half * (4 // nhalf), (half + 1) * (4 // nhalf)))
        for e in experts:
            if l == 3:
                P.dma("y", b1_sb.full, Ref(C.W["w3_b1"], C.W["w3_b1"].t[128 * e:128 * e + 128, :]))
                P.dma("y", eh.full, C.d_eh.ap(1024 * e, [[128, 8], [1, 128]]))
            for fq in range(4):
                bi = fq % nset
                for kc in range(8):
                    P.dma("g", w1t[bi][kc].full,
                          Ref(w1d, w1d.t[1024 * e + 128 * kc:1024 * e + 128 * kc + 128,
                                         1024 * fq:1024 * fq + 1024]))
                for fc in range(8):
                    r0 = 4096 * e + 1024 * fq + 128 * fc
                    P.dma("g", w2t[0][fc].full, Ref(w2d, w2d.t[r0:r0 + 128, :]))
                first_group = (e == experts[0] and fq == 0)
                for tg in tgs:
                    cols = slice(512 * tg, 512 * tg + 512)
                    lcols = slice(512 * tg - 2048 * half // 2 * nhalf // 1, 0) if False else \
                        slice(512 * (tg - tgs[0]), 512 * (tg - tgs[0]) + 512)
                    if l == 3:
                        gbt = gb[tg % 2]
                        P.mms(Ref(pb_or[0], pb_or[0].t[0:128, 0:512]),
                              [(eh.full, Ref(G, G.t[0:8, cols]))])
                        v_copy(P, gbt.full, Ref(pb_or[0], pb_or[0].t[0:128, 0:512]))
                    for fc in range(8):
                        Fg = 8 * fq + fc
                        pbh = pb_hr[hctr[0] % 2]
                        hctr[0] += 1
                        P.mms(pbh.full,
                              [(Ref(w1t[bi][kc], w1t[bi][kc].t[:, 128 * fc:128 * fc + 128]),
                                Ref(C.xno[kc], C.xno[kc].t[:, cols])) for kc in range(8)])
                        if l == 0:
                            v_stt(P, ht[fc].full, pbh.full,
                                  Ref(b1_sb, b1_sb.t[:, Fg:Fg + 1]),
                                  Ref(zeros16, zeros16.t[:, 0:512]), ALU.add, ALU.max)
                        else:
                            s_act(P, ht[fc].full, pbh.full, AF.Gelu,
                                  bias=Ref(b1_sb, b1_sb.t[:, Fg:Fg + 1]))
                            v_tt(P, ht[fc].full, ht[fc].full, gbt.full, ALU.mult)
                    for dc in range(8):
                        items = []
                        if first_group:
                            if l == 0:
                                items.append((Ref(b2r16, b2r16.t[0:1, 128 * dc:128 * dc + 128]),
                                              Ref(C.ones16, C.ones16.t[0:1, 0:512])))
                            else:
                                items.append((Ref(b216, b216.t[0:8, 128 * dc:128 * dc + 128]),
                                              Ref(G16, G16.t[0:8, cols])))
                        items += [(Ref(w2t[0][fc], w2t[0][fc].t[:, 128 * dc:128 * dc + 128]),
                                   ht[fc].full) for fc in range(8)]
                        pbo = pb_or[octr[0] % 2]
                        octr[0] += 1
                        P.mms(pbo.full, items)
                        ar = Ref(acc[dc], acc[dc].t[:, lcols])
                        if first_group:
                            v_copy(P, ar, pbo.full)
                        else:
                            v_tt(P, ar, ar, pbo.full, ALU.add)
                yield

        if l == 3:
            for tg in tgs:
                cols = slice(512 * tg, 512 * tg + 512)
                lcols = slice(512 * (tg - tgs[0]), 512 * (tg - tgs[0]) + 512)
                for c in range(8):
                    gg = g16[c % 2]
                    s_act(P, gg.full, Ref(acc[c], acc[c].t[:, lcols]), AF.Silu)
                    P.dma("y", Ref(C.g_dram, C.g_dram.t[128 * c:128 * c + 128, cols]), gg.full)
            yield
        else:
            x0t = [_mkbuf(P, ph, f"mo0_x0{i}", (128, 512), F32) for i in range(8)]
            sqt = [_mkbuf(P, ph, f"mo0_sq{i}", (128, 512), F32) for i in range(2)]
            tt1 = _mkbuf(P, ph, "mo0_t1", (128, 512), F32)
            tt2 = _mkbuf(P, ph, "mo0_t2", (128, 512), F32)
            bmu = _mkbuf(P, ph, "mo0_bmu", (128, 512), F32)
            brs = _mkbuf(P, ph, "mo0_brs", (128, 512), F32)
            gam_sb = _mkbuf(P, ph, "mo0_gam", (128, 8), F32)
            bet_sb = _mkbuf(P, ph, "mo0_bet", (128, 8), F32)
            P.dma("y", gam_sb.full, C.W["w0_gam"].full)
            P.dma("y", bet_sb.full, C.W["w0_bet"].full)
            for tg in range(4):
                cols = slice(512 * tg, 512 * tg + 512)
                _row_bcasts(P, C, bmu, brs, tg, C.pb[5])
                for c in range(8):
                    P.dma("y", tt1.full, C.x_src[c][:, cols])
                    v_tt(P, tt2.full, tt1.full, bmu.full, ALU.subtract)
                    v_tt(P, tt2.full, tt2.full, brs.full, ALU.mult)
                    v_tt(P, x0t[c].full, Ref(acc[c], acc[c].t[:, cols]), tt2.full, ALU.add)
                P.mms(C.pb[5].full, [(C.ones_f.full, x0t[c].full) for c in range(8)])
                for c in range(8):
                    sq = sqt[c % 2]
                    s_act(P, sq.full, x0t[c].full, AF.Square)
                    P.mms(C.pb[6].full, [(C.ones_f.full, sq.full)],
                          start=(c == 0), stop=(c == 7))
                v_ts(P, bmu.full, C.pb[5].full, 1.0 / 1024.0, ALU.mult)
                v_ts(P, tt1.full, C.pb[6].full, 1.0 / 1024.0, ALU.mult)
                v_tt(P, tt2.full, bmu.full, bmu.full, ALU.mult)
                v_tt(P, tt1.full, tt1.full, tt2.full, ALU.subtract)
                s_act(P, tt2.full, tt1.full, AF.Ln, bias=Ref(C.cst, C.cst.t[:, 0:1]))
                s_act(P, brs.full, tt2.full, AF.Exp, scale=-0.5)
                for c in range(8):
                    v_tt(P, tt1.full, x0t[c].full, bmu.full, ALU.subtract)
                    v_tt(P, tt1.full, tt1.full, brs.full, ALU.mult)
                    v_ts(P, tt2.full, tt1.full, Ref(gam_sb, gam_sb.t[:, c:c + 1]), ALU.mult,
                         Ref(bet_sb, bet_sb.t[:, c:c + 1]), ALU.add)
                    gg = g16[c % 2]
                    s_act(P, gg.full, tt2.full, AF.Silu)
                    P.dma("y", Ref(C.g_dram, C.g_dram.t[128 * c:128 * c + 128, cols]), gg.full)
                yield


def _emit_gates(P, C, ph, G, G16):
    rw16 = _mkbuf(P, ph, "g3_rw", (128, 64), BF16)
    rbr = _mkbuf(P, ph, "g3_rbr", (1, 8), F32)
    rbB = _mkbuf(P, ph, "g3_rbB", (128, 8), F32)
    lg = _mkbuf(P, ph, "g3_lg", (128, 8), F32)
    mskt = _mkbuf(P, ph, "g3_msk", (128, 8), F32)
    eq1 = _mkbuf(P, ph, "g3_eq1", (128, 8), F32)
    eq2 = _mkbuf(P, ph, "g3_eq2", (128, 8), F32)
    gt = _mkbuf(P, ph, "g3_gt", (128, 8), F32)
    colv = _mkbuf(P, ph, "g3_col", (128, 8), F32)
    P.dma("g", rw16.full, C.W["w3_rw"].full)
    P.dma("y", rbr.full, C.W["w3_rbr"].full)
    _bcast_row(P, C, Ref(rbB, rbB.t[0:128, 0:8]), rbr.full, C.pb[6], 8)
    m1 = Ref(colv, colv.t[:, 0:1])
    nm1 = Ref(colv, colv.t[:, 1:2])
    m2 = Ref(colv, colv.t[:, 2:3])
    ex = Ref(colv, colv.t[:, 3:4])
    dn = Ref(colv, colv.t[:, 4:5])
    g1 = Ref(colv, colv.t[:, 5:6])
    g2 = Ref(colv, colv.t[:, 6:7])
    for i in range(16):
        tcols = slice(128 * i, 128 * i + 128)
        P.mms(Ref(C.pb[5], C.pb[5].t[0:128, 0:8]),
              [(Ref(C.xno[kc], C.xno[kc].t[:, tcols]),
                Ref(rw16, rw16.t[:, 8 * kc:8 * kc + 8])) for kc in range(8)])
        v_tt(P, lg.full, Ref(C.pb[5], C.pb[5].t[0:128, 0:8]), rbB.full, ALU.add)
        v_red(P, m1, lg.full)
        v_ts(P, nm1, m1, -1.0, ALU.mult)
        v_ts(P, eq1.full, lg.full, m1, ALU.is_equal)
        v_stt(P, mskt.full, eq1.full, -1e30, lg.full, ALU.mult, ALU.add)
        v_red(P, m2, mskt.full)
        s_act(P, ex, m2, AF.Exp, bias=nm1)
        v_ts(P, dn, ex, 1.0, ALU.add)
        P.op("V", [dn], [g1], lambda e, g1=g1, dn=dn: e.reciprocal(g1.ap, dn.ap))
        v_tt(P, g2, ex, g1, ALU.mult)
        v_ts(P, eq2.full, mskt.full, m2, ALU.is_equal)
        v_ts(P, eq1.full, eq1.full, g1, ALU.mult)
        v_ts(P, eq2.full, eq2.full, g2, ALU.mult)
        v_tt(P, gt.full, eq1.full, eq2.full, ALU.add)
        P.transpose(Ref(C.pb[6], C.pb[6].t[0:8, 0:128]), gt.full, C.eye_f.full)
        v_copy(P, Ref(G, G.t[0:8, tcols]), Ref(C.pb[6], C.pb[6].t[0:8, 0:128]))
    v_copy(P, G16.full, G.full)


def emit_ffn_l1(P, C, ph):
    rm16 = [_mkbuf(P, ph, f"f1_rm{i}", (128, 128), BF16) for i in range(8)]
    rup16 = [_mkbuf(P, ph, f"f1_ru{i}", (128, 128), BF16) for i in range(32)]
    hact = [_mkbuf(P, ph, f"f1_h{i}", (128, 256), BF16) for i in range(128)]
    y16 = [_mkbuf(P, ph, f"f1_y{i}", (128, 256), BF16) for i in range(2)]
    cpm = _mkbuf(P, ph, "f1_cpm", (128, 16384), BF16)
    g16 = [_mkbuf(P, ph, f"f1_g16{i}", (128, 256), BF16) for i in range(2)]
    pby = (C.pb[4], C.pb[7])
    pbz = (C.pb[5], C.pb[6])
    pbc = (C.pb[4], C.pb[7])
    zc = [0]
    for h in range(4):
        for rg in range(8):
            P.dma("g", rm16[rg].full,
                  Ref(C.W["w1_rm"], C.W["w1_rm"].t[128 * (8 * h + rg):128 * (8 * h + rg) + 128, :]))
        for t in range(32):
            P.dma("g", rup16[t].full,
                  Ref(C.W["w1_rup"], C.W["w1_rup"].t[128 * (32 * h + t):128 * (32 * h + t) + 128, :]))
        for tg in range(8):
            cols = slice(256 * tg, 256 * tg + 256)
            for rg in range(8):
                yr = y16[rg % 2]
                pb0 = pby[rg % 2]
                P.mms(Ref(pb0, pb0.t[:, 0:256]),
                      [(rm16[rg].full, Ref(C.xno[rg], C.xno[rg].t[:, cols]))])
                v_copy(P, yr.full, Ref(pb0, pb0.t[:, 0:256]))
                for i in range(4):
                    pb1 = pbz[zc[0] % 2]
                    zc[0] += 1
                    P.mms(Ref(pb1, pb1.t[:, 0:256]), [(rup16[rg * 4 + i].full, yr.full)])
                    s_act(P, hact[h * 32 + rg * 4 + i].full, Ref(pb1, pb1.t[:, 0:256]), AF.Gelu)
            yield
    for mc in range(8):
        P.dma("g", cpm.full,
              Ref(C.W["w1_cpT"], C.W["w1_cpT"].t[128 * mc:128 * mc + 128, :]))
        for tg in range(8):
            cols = slice(256 * tg, 256 * tg + 256)
            pb2 = pbc[tg % 2]
            P.mms(Ref(pb2, pb2.t[:, 0:256]),
                  [(Ref(cpm, cpm.t[:, 128 * kc:128 * kc + 128]), hact[kc].full)
                   for kc in range(128)])
            gg = g16[tg % 2]
            s_act(P, gg.full, Ref(pb2, pb2.t[:, 0:256]), AF.Silu)
            P.dma("y", Ref(C.g_dram, C.g_dram.t[128 * mc:128 * mc + 128, cols]), gg.full)
        yield


def emit_ffn_l2(P, C, ph):
    out = [_mkbuf(P, ph, f"f2_o{i}", (128, 1024), F32) for i in range(8)]
    o16 = [_mkbuf(P, ph, f"f2_o16{i}", (128, 512), BF16) for i in range(8)]
    cw1 = [_mkbuf(P, ph, f"f2_w1{i}", (128, 1024), BF16) for i in range(8)]
    cw2 = [_mkbuf(P, ph, f"f2_w2{i}", (128, 512), BF16) for i in range(8)]
    cw3 = [_mkbuf(P, ph, f"f2_w3{i}", (128, 256), BF16) for i in range(4)]
    cw4 = [_mkbuf(P, ph, f"f2_w4{i}", (128, 1), BF16) for i in range(2)]
    c1 = [_mkbuf(P, ph, f"f2_c1{i}", (128, 512), BF16) for i in range(8)]
    c2 = [_mkbuf(P, ph, f"f2_c2{i}", (128, 512), BF16) for i in range(4)]
    c3 = [_mkbuf(P, ph, f"f2_c3{i}", (128, 512), BF16) for i in range(2)]
    cb1 = _mkbuf(P, ph, "f2_cb1", (128, 8), F32)
    cb2 = _mkbuf(P, ph, "f2_cb2", (128, 4), F32)
    cb3 = _mkbuf(P, ph, "f2_cb3", (128, 2), F32)
    cb4 = _mkbuf(P, ph, "f2_cb4", (1, 1), F32)
    ar = _mkbuf(P, ph, "f2_ar", (1, 1), F32)
    srow = _mkbuf(P, ph, "f2_srow", (1, 4), F32)
    tfr = _mkbuf(P, ph, "f2_tfr", (1, 512), F32)
    cbt = _mkbuf(P, ph, "f2_cbt", (128, 512), F32)
    tmpt = _mkbuf(P, ph, "f2_tmp", (128, 512), F32)
    t1t = _mkbuf(P, ph, "f2_t1", (128, 512), F32)
    t2t = t1t
    bmu = _mkbuf(P, ph, "f2_bmu", (128, 512), F32)
    brs = _mkbuf(P, ph, "f2_brs", (128, 512), F32)
    g16 = [_mkbuf(P, ph, f"f2_g16{i}", (128, 512), BF16) for i in range(2)]
    for kc in range(8):
        P.dma("g", cw1[kc].full, Ref(C.W["w2_cw1"], C.W["w2_cw1"].t[128 * kc:128 * kc + 128, :]))
        P.dma("g", cw2[kc].full, Ref(C.W["w2_cw2"], C.W["w2_cw2"].t[128 * kc:128 * kc + 128, :]))
    for kc in range(4):
        P.dma("g", cw3[kc].full, Ref(C.W["w2_cw3"], C.W["w2_cw3"].t[128 * kc:128 * kc + 128, :]))
    for kc in range(2):
        P.dma("g", cw4[kc].full, Ref(C.W["w2_cw4"], C.W["w2_cw4"].t[128 * kc:128 * kc + 128, :]))
    P.dma("y", cb1.full, C.W["w2_cb1"].full)
    P.dma("y", cb2.full, C.W["w2_cb2"].full)
    P.dma("y", cb3.full, C.W["w2_cb3"].full)
    P.dma("y", cb4.full, C.W["w2_cb4"].full)
    P.dma("y", ar.full, C.W["w2_ar"].full)
    v_ts(P, Ref(srow, srow.t[0:1, 0:1]), C.gf.full, 0.7, ALU.mult)
    v_ts(P, Ref(srow, srow.t[0:1, 0:1]), Ref(srow, srow.t[0:1, 0:1]), -0.5, ALU.add)
    v_ts(P, Ref(srow, srow.t[0:1, 1:2]), ar.full, 2.0, ALU.mult)
    pbAr = (C.pb[4], C.pb[7])
    pbB = C.pb[5]
    pbX = C.pb[6]
    ac = [0]
    pairs = [((0, 1), (4, 5), ALU.add, ALU.subtract),
             ((2, 3), (6, 7), ALU.subtract, ALU.add)]
    for half in range(2):
        tgs = [2 * half, 2 * half + 1]
        for tg in tgs:
            cols = slice(512 * tg, 512 * tg + 512)
            lcols = slice(512 * (tg - tgs[0]), 512 * (tg - tgs[0]) + 512)
            _row_bcasts(P, C, bmu, brs, tg, pbB)
            for c in range(8):
                P.dma("y", tmpt.full, C.x_src[c][:, cols])
                v_tt(P, t1t.full, tmpt.full, bmu.full, ALU.subtract)
                v_tt(P, Ref(out[c], out[c].t[:, lcols]), t1t.full, brs.full, ALU.mult)
        for cyc in range(3):
            for tg in tgs:
                cols = slice(512 * tg, 512 * tg + 512)
                lcols = slice(512 * (tg - tgs[0]), 512 * (tg - tgs[0]) + 512)
                for c in range(8):
                    v_copy(P, o16[c].full, Ref(out[c], out[c].t[:, lcols]))
                for mc in range(8):
                    pbA = pbAr[ac[0] % 2]
                    ac[0] += 1
                    P.mms(pbA.full, [(Ref(cw1[kc], cw1[kc].t[:, 128 * mc:128 * mc + 128]),
                                      o16[kc].full) for kc in range(8)])
                    s_act(P, c1[mc].full, pbA.full, AF.Gelu, bias=Ref(cb1, cb1.t[:, mc:mc + 1]))
                for mc in range(4):
                    pbA = pbAr[ac[0] % 2]
                    ac[0] += 1
                    P.mms(pbA.full, [(Ref(cw2[kc], cw2[kc].t[:, 128 * mc:128 * mc + 128]),
                                      c1[kc].full) for kc in range(8)])
                    s_act(P, c2[mc].full, pbA.full, AF.Gelu, bias=Ref(cb2, cb2.t[:, mc:mc + 1]))
                for mc in range(2):
                    pbA = pbAr[ac[0] % 2]
                    ac[0] += 1
                    P.mms(pbA.full, [(Ref(cw3[kc], cw3[kc].t[:, 128 * mc:128 * mc + 128]),
                                      c2[kc].full) for kc in range(4)])
                    s_act(P, c3[mc].full, pbA.full, AF.Gelu, bias=Ref(cb3, cb3.t[:, mc:mc + 1]))
                P.mms(Ref(pbB, pbB.t[0:1, 0:512]),
                      [(Ref(cw4[kc], cw4[kc].t[:, 0:1]), c3[kc].full) for kc in range(2)])
                s_act(P, tfr.full, Ref(pbB, pbB.t[0:1, 0:512]), AF.Sigmoid,
                      bias=Ref(cb4, cb4.t[0:1, 0:1]))
                v_ts(P, tfr.full, tfr.full, 0.3, ALU.mult)
                v_ts(P, tfr.full, tfr.full, Ref(srow, srow.t[0:1, 0:1]), ALU.add)
                v_ts(P, tfr.full, tfr.full, Ref(srow, srow.t[0:1, 1:2]), ALU.mult)
                v_ts(P, tfr.full, tfr.full, 0.1, ALU.add)
                _bcast_row(P, C, cbt.full, tfr.full, pbX, 512)
                for (A, Bq, sA, sB) in pairs:
                    for j in range(2):
                        ca, cb_ = A[j], Bq[j]
                        v_copy(P, tmpt.full, Ref(out[ca], out[ca].t[:, lcols]))
                        v_tt(P, t1t.full, cbt.full, Ref(out[cb_], out[cb_].t[:, lcols]), ALU.mult)
                        v_tt(P, Ref(out[ca], out[ca].t[:, lcols]),
                             Ref(out[ca], out[ca].t[:, lcols]), t1t.full, sA)
                        v_tt(P, t2t.full, cbt.full, tmpt.full, ALU.mult)
                        v_tt(P, Ref(out[cb_], out[cb_].t[:, lcols]),
                             Ref(out[cb_], out[cb_].t[:, lcols]), t2t.full, sB)
                yield
        for tg in tgs:
            cols = slice(512 * tg, 512 * tg + 512)
            lcols = slice(512 * (tg - tgs[0]), 512 * (tg - tgs[0]) + 512)
            for c in range(8):
                gg = g16[c % 2]
                s_act(P, gg.full, Ref(out[c], out[c].t[:, lcols]), AF.Silu)
                P.dma("y", Ref(C.g_dram, C.g_dram.t[128 * c:128 * c + 128, cols]), gg.full)
        yield



WNAMES = {
    0: ["w0_qT", "w0_kT", "w0_vT", "w0_bq", "w0_bk", "w0_bv", "w0_bvr",
        "w0_w1", "w0_w2", "w0_b1", "w0_b2r", "w0_gam", "w0_bet",
        "mt0_r0", "mt0_r1"],
    1: ["w1_qkvT", "w1_woT", "w1_rm", "w1_rup", "w1_cpT", "mt1_r0", "mt1_r1"],
    2: ["w2_qkvT", "w2_woT", "w2_bo", "w2_cw1", "w2_cw2", "w2_cw3", "w2_cw4",
        "w2_cb1", "w2_cb2", "w2_cb3", "w2_cb4", "w2_gw1", "w2_gw2", "w2_gw3",
        "w2_gb1", "w2_gb2", "w2_gb3", "w2_ar", "mt2_r0", "mt2_r1"],
    3: ["w3_q", "w3_k", "w3_bq", "w3_bk", "w3_bkr", "w3_rw", "w3_rbr",
        "w3_w1", "w3_w2", "w3_b1", "w3_b2"],
}
CONST_NAMES = ["c_ones_f", "c_eye_f", "c_bd64", "c_ones16", "c_zeros16",
               "c_eh", "c_cst"]


def _np2dt(a):
    return BF16 if a.dtype == BF else F32


def build_program(layers, prep0):
    nc = bass.Bass("TRN2", target_bir_lowering=False)
    P = Prog(nc)
    C = Ctx()
    C.W = {}
    innames = []

    x_in = P.dram("x_in", [1024, 2048], F32, kind="ExternalInput")
    innames.append("x_in")
    x_out = P.dram("x_out", [1024, 2048], F32, kind="ExternalOutput")
    xmid = [P.dram(f"x_mid{i}", [1024, 2048], F32) for i in range(len(layers) - 1)]

    for nm in CONST_NAMES:
        innames.append(nm)
    for l in layers:
        for nm in WNAMES[l]:
            a = prep0[nm]
            C.W[nm] = P.dram(nm, list(a.shape), _np2dt(a), kind="ExternalInput")
            innames.append(nm)

    _load_consts(P, C)

    C.pb = [P.psum(f"pb{i}") for i in range(8)]
    C.xno = [P.sbuf(f"xno{c}", [128, 2048], BF16) for c in range(8)]
    C.r_dram = P.dram("r_scr", [2, 2048], F32)
    C.rowscr = P.sbuf("rowscr", [1, 512], F32)
    C.u_x = [P.sbuf(f"u_x{i}", [128, 512], F32) for i in range(3)]
    C.u_s = [P.sbuf(f"u_s{i}", [128, 512], F32) for i in range(3)]
    C.u_g = [P.sbuf(f"u_g{i}", [128, 512], BF16) for i in range(3)]

    if 2 in layers:
        C.gf = P.sbuf("gf", [1, 1], F32)

    C.gin = P.dram("gin", [1024, 2048], BF16)
    C.gout = P.dram("gout", [2048, 2048], BF16)
    C.s_dram = P.dram("s_scr", [1032, 2048], F32)
    C.q_dram = P.dram("q_scr", [1024, 2048], BF16)
    C.g_dram = P.dram("g_scr", [1024, 2048], BF16)
    C.o_dram = P.dram("o_scr", [1024, 2048], F32)

    for li, l in enumerate(layers):
        src = x_in if li == 0 else xmid[li - 1]
        dst = x_out if li == len(layers) - 1 else xmid[li]
        C.x_src = [Chunk(src, 128 * c) for c in range(8)]
        C.x_out = dst
        if l == 0:
            C.mtab = (C.W["mt0_r0"], C.W["mt0_r1"])
        elif l == 1:
            C.mtab = (C.W["mt1_r0"], C.W["mt1_r1"])
        elif l == 2:
            C.mtab = (C.W["mt2_r0"], C.W["mt2_r1"])
        else:
            C.mtab = None
        import contextlib
        lst = contextlib.ExitStack()
        if l == 0:
            C.recip_l0 = [_mkbuf(P, lst, f"recip0_{i}", (128, 512), F32)
                          for i in range(4)]
        emit_ln(P, C, l)
        emit_gather(P, C)
        ph_a = contextlib.ExitStack()
        ph_f = contextlib.ExitStack()
        if l == 0:
            emit_attn_l0(P, C)
            for _ in emit_moe(P, C, 0, ph_f):
                pass
        elif l == 1:
            drain_interleaved([(emit_attn_mh(P, C, 1, ph_a), 16),
                               (emit_ffn_l1(P, C, ph_f), 40)], head=[3.0, 0.0])
        elif l == 2:
            gfst = contextlib.ExitStack()
            emit_gf(P, C, gfst)
            drain_interleaved([(emit_attn_mh(P, C, 2, ph_a), 16),
                               (emit_ffn_l2(P, C, ph_f), 14)], head=[2.0, 0.0])
        else:
            drain_interleaved([(emit_attn_mh(P, C, 3, ph_a), 8),
                               (emit_moe(P, C, 3, ph_f), 66)], head=[1.5, 0.0])
        emit_updates(P, C, l)
        P.barrier()
        ph_a.close()
        ph_f.close()
        if l == 2:
            gfst.close()
        lst.close()

    P.finish()
    P.build_block()
    return nc, innames


_prog_cache = {}


def _get_program(layers, prep0):
    key = tuple(layers)
    if key not in _prog_cache:
        _prog_cache[key] = build_program(layers, prep0)
    return _prog_cache[key]


def kernel(**inputs):
    percore = host_prep(inputs)
    xcur = [m["x_in"] for m in percore]
    stages = [[0, 1, 2, 3]]
    for layers in stages:
        nc, innames = _get_program(layers, percore[0])
        in_maps = []
        for c in range(NCORE):
            im = {n: percore[c][n] for n in innames if n != "x_in"}
            im["x_in"] = np.ascontiguousarray(xcur[c])
            in_maps.append(im)
        res = run_bass_kernel_spmd(nc, in_maps, core_ids=list(range(NCORE)))
        xcur = [np.asarray(res.results[c]["x_out"]) for c in range(NCORE)]
    return host_finish(xcur)


if __name__ == "__main__":
    import reference as _r
    inp = _r.setup_inputs()
    inp = {k: np.asarray(v) for k, v in inp.items()}
    out = kernel(**inp)
    ref = np.asarray(_r.reference(**{k: v for k, v in inp.items()}))
    num = float(np.mean((out - ref) ** 2))
    den = float(np.mean(ref ** 2))
    print("resid var ratio:", num / den)


# version 11
# speedup vs baseline: 1.1814x; 1.1814x over previous
import numpy as np
import ml_dtypes
import concourse.bass as bass
import concourse.mybir as mybir
from concourse.bass_utils import run_bass_kernel_spmd

F32 = mybir.dt.float32
BF16 = mybir.dt.bfloat16
AF = mybir.ActivationFunctionType
ALU = mybir.AluOpType
AX = mybir.AxisListType
BF = ml_dtypes.bfloat16

NCORE = 8
D = 1024
SEQ = 4096
NB = 4
TOK = 2048
NT = 16
NC = 8
MW = 8320
MC = 4096

ATTN_SCALE = {0: 1.0 / 1024.0, 1: 0.125, 2: 0.125, 3: 0.03125}



def _percol(v, p=128):
    v = np.asarray(v, dtype=np.float32)
    return np.ascontiguousarray(v.reshape(-1, p).T)


def _masktab(mv, shift):
    j = np.arange(MW)[None, :]
    p = np.arange(128)[:, None]
    d = j - MC - p + shift
    return mv(d).astype(BF)


def _mv1(d):
    w = (d >= 0) & (d <= 127)
    dil = (d >= 132) & ((d - 128) % 4 == 0) & (d <= 4095)
    return (w | dil).astype(np.float32)


def _mv2(d):
    return ((d >= 0) & (d <= 4095)).astype(np.float32)


def _mv0(d):
    return (d != 0).astype(np.float32)


def host_prep(inp):
    f32 = lambda a: np.ascontiguousarray(np.asarray(a, dtype=np.float32))
    shared = {}
    shared["c_ones_f"] = np.ones((128, 128), np.float32)
    shared["c_eye_f"] = np.eye(128, dtype=np.float32)
    bd = np.zeros((128, 2), np.float32)
    bd[0:64, 0] = 1.0
    bd[64:128, 1] = 1.0
    shared["c_bd64"] = bd
    shared["c_ones16"] = np.ones((128, 512), BF)
    shared["c_zeros16"] = np.zeros((128, 512), BF)
    eh = np.zeros((8, 8, 128), np.float32)
    for e in range(8):
        eh[e, e, :] = 1.0
    shared["c_eh"] = eh
    cst = np.zeros((128, 8), np.float32)
    cst[:, 0] = 1e-5
    cst[:, 2] = 1.0
    shared["c_cst"] = cst

    shared["w0_qT"] = f32(inp["L0_attn_Wq"].T)
    shared["w0_kT"] = f32(inp["L0_attn_Wk"].T)
    shared["w0_vT"] = f32(inp["L0_attn_Wv"].T)
    shared["w0_bq"] = _percol(inp["L0_attn_bq"])
    shared["w0_bk"] = _percol(inp["L0_attn_bk"])
    shared["w0_bv"] = _percol(inp["L0_attn_bv"])
    shared["w0_bvr"] = f32(inp["L0_attn_bv"]).reshape(1, 1024)
    shared["w0_w1"] = f32(inp["L0_ffn_w1"][0])
    shared["w0_w2"] = f32(inp["L0_ffn_w2"][0])
    shared["w0_b1"] = _percol(inp["L0_ffn_b1"][0])
    shared["w0_b2r"] = f32(inp["L0_ffn_b2"][0]).reshape(1, 1024)
    shared["w0_gam"] = _percol(inp["L0_ffn_gamma"])
    shared["w0_bet"] = _percol(inp["L0_ffn_beta"])
    shared["w1_qkvT"] = f32(inp["L1_attn_qkv"].T)
    shared["w1_woT"] = f32(inp["L1_attn_wo"].T)
    rm = np.asarray(inp["L1_ffn_rail_maker"], np.float32)
    rmbd = np.zeros((4, 8, 128, 128), np.float32)
    for h in range(4):
        for rg in range(8):
            for u in range(8):
                rmbd[h, rg, 16 * u:16 * u + 16, 16 * u:16 * u + 16] = rm[h, 8 * rg + u]
    shared["w1_rm"] = np.ascontiguousarray(rmbd.reshape(4096, 128))
    rup = np.asarray(inp["L1_ffn_rail_up_proj"], np.float32)
    rupbd = np.zeros((4, 8, 4, 128, 128), np.float32)
    for h in range(4):
        for rg in range(8):
            for i in range(4):
                for u in range(2):
                    r = 8 * rg + 2 * i + u
                    rupbd[h, rg, i, 16 * (2 * i + u):16 * (2 * i + u) + 16,
                          64 * u:64 * u + 64] = rup[h, r].T
    shared["w1_rup"] = np.ascontiguousarray(rupbd.reshape(16384, 128))
    cpT = f32(inp["L1_ffn_w_cproj"].T)
    shared["w1_cpT"] = np.ascontiguousarray(
        cpT.reshape(128, 128, 8, 128).transpose(2, 1, 0, 3).reshape(1024, 16384))
    wqkvT = f32(inp["L2_attn_w_qkv"].T)
    g = np.zeros_like(wqkvT)
    for h in range(16):
        g[:, 64 * h:64 * h + 64] = wqkvT[:, 192 * h:192 * h + 64]
        g[:, 1024 + 64 * h:1024 + 64 * h + 64] = wqkvT[:, 192 * h + 64:192 * h + 128]
        g[:, 2048 + 64 * h:2048 + 64 * h + 64] = wqkvT[:, 192 * h + 128:192 * h + 192]
    shared["w2_qkvT"] = g
    shared["w2_woT"] = f32(inp["L2_attn_w_proj"].T)
    shared["w2_bo"] = _percol(inp["L2_attn_b_proj"])
    shared["w2_cw1"] = f32(inp["L2_ffn_cn_w1"])
    shared["w2_cw2"] = f32(inp["L2_ffn_cn_w2"])
    shared["w2_cw3"] = f32(inp["L2_ffn_cn_w3"])
    shared["w2_cw4"] = f32(inp["L2_ffn_cn_w4"])
    shared["w2_cb1"] = _percol(inp["L2_ffn_cn_b1"])
    shared["w2_cb2"] = _percol(inp["L2_ffn_cn_b2"])
    shared["w2_cb3"] = _percol(inp["L2_ffn_cn_b3"])
    shared["w2_cb4"] = f32(inp["L2_ffn_cn_b4"]).reshape(1, 1)
    shared["w2_gw1"] = f32(inp["L2_ffn_gc_w1"])
    shared["w2_gw2"] = f32(inp["L2_ffn_gc_w2"])
    shared["w2_gw3"] = f32(inp["L2_ffn_gc_w3"])
    shared["w2_gb1"] = _percol(inp["L2_ffn_gc_b1"])
    shared["w2_gb2"] = _percol(inp["L2_ffn_gc_b2"])
    shared["w2_gb3"] = f32(inp["L2_ffn_gc_b3"]).reshape(1, 1)
    shared["w2_ar"] = f32(inp["L2_ffn_adaptive_range"]).reshape(1, 1)
    shared["w3_q"] = f32(inp["L3_attn_Wq"])
    shared["w3_k"] = f32(inp["L3_attn_Wk"])
    shared["w3_bq"] = _percol(inp["L3_attn_bq"])
    shared["w3_bk"] = _percol(inp["L3_attn_bk"])
    shared["w3_bkr"] = f32(inp["L3_attn_bk"]).reshape(1, 1024)
    rw = f32(inp["L3_ffn_router_w"])
    shared["w3_rw"] = np.ascontiguousarray(
        rw.reshape(8, 128, 8).transpose(1, 0, 2).reshape(128, 64))
    shared["w3_rbr"] = f32(inp["L3_ffn_router_b"]).reshape(1, 8)
    shared["w3_w1"] = f32(inp["L3_ffn_w1"]).reshape(8192, 4096)
    shared["w3_w2"] = f32(inp["L3_ffn_w2"]).reshape(32768, 1024)
    b1 = f32(inp["L3_ffn_b1"])
    shared["w3_b1"] = np.ascontiguousarray(
        b1.reshape(8, 32, 128).transpose(0, 2, 1).reshape(1024, 32))
    shared["w3_b2"] = f32(inp["L3_ffn_b2"])

    x0 = np.asarray(inp["L0_attn_x"], np.float32)
    percore = []
    for c in range(NCORE):
        b, par = c // 2, c % 2
        m = dict(shared)
        xt = x0[b].reshape(32, 128, D)[par::2].reshape(TOK, D)
        m["x_in"] = np.ascontiguousarray(xt.T)
        m["mt1_r0"] = _masktab(_mv1, 128 * (par - 0))
        m["mt1_r1"] = _masktab(_mv1, 128 * (par - 1))
        m["mt2_r0"] = _masktab(_mv2, 128 * (par - 0))
        m["mt2_r1"] = _masktab(_mv2, 128 * (par - 1))
        m["mt0_r0"] = _masktab(_mv0, 128 * (par - 0))
        m["mt0_r1"] = _masktab(_mv0, 128 * (par - 1))
        percore.append(m)
    return percore


def host_finish(outs):
    out = np.empty((NB, SEQ, D), np.float32)
    for c in range(NCORE):
        b, par = c // 2, c % 2
        out[b].reshape(32, 128, D)[par::2] = outs[c].T.reshape(NT, 128, D)
    return out


def q_g0(par, a):
    return (2 * a + par) * 128


def k_g0(r, kb):
    return (2 * (kb % 16) + r) * 128


def tile_live(l, qg, kb):
    if l in (0, 3):
        return True
    r = kb // 16
    for par in (0, 1):
        qmax = q_g0(par, 4 * qg + 3) + 127
        kmin = k_g0(r, kb)
        if qmax >= kmin:
            return True
    return False


def tile_needs_mask(l, qg, kb):
    if l == 1:
        return True
    r = kb // 16
    if l == 2:
        for par in (0, 1):
            qmin = q_g0(par, 4 * qg)
            kmax = k_g0(r, kb) + 127
            if qmin < kmax:
                return True
        return False
    if l == 0:
        if r != 0 and True:
            pass
        for par in (0, 1):
            for a in range(4):
                d0 = 256 * (4 * qg + a - (kb % 16)) + 128 * (par - r)
                if -511 <= d0 <= 127:
                    return True
        return False
    return False



class Ref:
    __slots__ = ("buf", "ap")

    def __init__(self, buf, ap):
        self.buf = buf
        self.ap = ap


class Buf:
    __slots__ = ("t", "name", "w", "r")

    def __init__(self, t, name):
        self.t = t
        self.name = name
        self.w = {}
        self.r = {}

    def __getitem__(self, idx):
        return Ref(self, self.t[idx])

    def ap(self, offset, dims):
        return Ref(self, bass.AP(self.t, offset, dims))

    @property
    def full(self):
        return Ref(self, self.t[:, :])


ENGS = ("T", "V", "S", "G", "Y")


NPOOL = 12


class Prog:
    def __init__(self, nc):
        self.nc = nc
        self.ops = {e: [] for e in ENGS}
        self.cnt = {"T": 0, "V": 0, "S": 0, "G": 0}
        self.qissue = {"y": 0, "g": 0}
        self.quse = {("y", i): 0 for i in range(NPOOL)}
        self.quse.update({("g", i): 0 for i in range(NPOOL)})
        self.cc = 0
        self.waited = {e: {} for e in ENGS}
        self._sb_stack = []

    def sbuf(self, name, shape, dtype):
        t = self.nc.alloc_sbuf_tensor(name, shape, dtype)
        return Buf(t, name)

    def psum(self, name, shape=(128, 512), dtype=F32):
        t = self.nc.alloc_psum_tensor(name, list(shape), dtype)
        return Buf(t, name)

    def dram(self, name, shape, dtype, kind="Internal"):
        t = self.nc.dram_tensor(name, list(shape), dtype, kind=kind)
        return Buf(t, name)

    def _wait(self, eng, key, val):
        if val <= 0:
            return
        if self.waited[eng].get(key, 0) >= val:
            return
        self.waited[eng][key] = val
        sems = None

        def w(e, key=key, val=val):
            e.wait_ge(self.sems[key], val)
        self.ops[eng].append(w)

    def _pre(self, eng, ins, outs):
        for ref in ins:
            for k, v in ref.buf.w.items():
                self._wait(eng, k, v)
        for ref in outs:
            for k, v in ref.buf.w.items():
                self._wait(eng, k, v)
            for k, v in ref.buf.r.items():
                self._wait(eng, k, v)

    def _post(self, key, cnt, ins, outs):
        for ref in outs:
            ref.buf.w = {key: cnt}
            ref.buf.r = {}
        for ref in ins:
            if ref.buf.r.get(key, 0) < cnt:
                ref.buf.r[key] = cnt

    def op(self, eng, ins, outs, fn):
        assert eng in ("T", "V", "S", "G")
        self._pre(eng, ins, outs)
        self.cnt[eng] += 1
        cnt = self.cnt[eng]

        def run(e, fn=fn, eng=eng, cnt=cnt):
            inst = fn(e)
            inst.then_inc(self.sems[eng], 1)
        self.ops[eng].append(run)
        self._post(eng, cnt, ins, outs)

    def mms(self, psum_ref, items, extra_ins=(), tile_position=None,
            start=True, stop=True):
        ins = [it[0] for it in items] + [it[1] for it in items] + list(extra_ins)
        self._pre("T", ins, [psum_ref])
        self.cnt["T"] += 1
        cnt = self.cnt["T"]
        n = len(items)

        def run(e, items=items, psum_ref=psum_ref, n=n, tp=tile_position,
                start=start, stop=stop):
            for i, (lh, rh) in enumerate(items):
                inst = e.matmul(psum_ref.ap, lh.ap, rh.ap,
                                start=(start and i == 0),
                                stop=(stop and i == n - 1),
                                tile_position=tp)
            inst.then_inc(self.sems["T"], 1)
        self.ops["T"].append(run)
        self._post("T", cnt, ins, [psum_ref])

    def transpose(self, psum_ref, in_ref, ident_ref):
        self.op("T", [in_ref, ident_ref], [psum_ref],
                lambda e: e.transpose(psum_ref.ap, in_ref.ap, ident_ref.ap))

    def dma(self, queue, out, in_, **kw):
        eng = "Y" if queue == "y" else "G"
        accum = kw.get("accum_op") is not None
        ins = [in_] + ([out] if accum else [])
        self._pre(eng, ins, [out])
        sidx = self.qissue[queue] % NPOOL
        self.qissue[queue] += 1
        key = (queue, sidx)
        self._wait(eng, key, self.quse[key])
        self.quse[key] += 16
        cnt = self.quse[key]

        kw = {k: v for k, v in kw.items() if v is not None}

        def run(e, out=out, in_=in_, kw=kw, key=key):
            e.dma_start(out=out.ap, in_=in_.ap, **kw).then_inc(self.sems[key], 16)
        self.ops[eng].append(run)
        self._post(key, cnt, [in_], [out])

    def coll(self, kind, op, groups, in_ref, out_ref):
        self._pre("G", [in_ref], [out_ref])
        self.cc += 1
        cnt = self.cc

        def run(e, kind=kind, op=op, groups=groups, in_ref=in_ref, out_ref=out_ref):
            e.collective_compute(kind, op, replica_groups=groups,
                                 ins=[in_ref.ap.opt()],
                                 outs=[out_ref.ap.opt()]).then_inc(self.sems["cc"], 1)
        self.ops["G"].append(run)
        self._post("cc", cnt, [in_ref], [out_ref])

    def barrier(self):
        tgts = {e: self.cnt[e] for e in ("T", "V", "S", "G")}
        tgts.update(self.quse)
        tgts["cc"] = self.cc
        for eng in ENGS:
            for k, v in tgts.items():
                self._wait(eng, k, v)

    def finish(self):
        self.barrier()

    def build_block(self):
        nc = self.nc
        import contextlib
        st = contextlib.ExitStack()
        self.sems = {}
        for name in ("T", "V", "S", "G", "cc"):
            self.sems[name] = st.enter_context(nc.semaphore("sem_" + name))
        for q in ("y", "g"):
            for i in range(NPOOL):
                self.sems[(q, i)] = st.enter_context(nc.semaphore(f"sem_{q}{i}"))
        blk = st.enter_context(nc.Block())
        ops = self.ops

        def mk(engname):
            def body(e):
                for f in ops[engname]:
                    f(e)
            return body
        blk.tensor(mk("T"))
        blk.vector(mk("V"))
        blk.scalar(mk("S"))
        blk.gpsimd(mk("G"))
        blk.sync(mk("Y"))
        st.close()



def v_copy(P, out, in_):
    P.op("V", [in_], [out], lambda e: e.tensor_copy(out.ap, in_.ap))


def v_tt(P, out, a, b, alu):
    P.op("V", [a, b], [out], lambda e: e.tensor_tensor(out.ap, a.ap, b.ap, alu))


def v_ts(P, out, in_, s1, op0, s2=None, op1=None, extra=()):
    ins = [in_] + [s for s in (s1, s2) if isinstance(s, Ref)] + list(extra)
    a1 = s1.ap if isinstance(s1, Ref) else s1
    a2 = s2.ap if isinstance(s2, Ref) else s2
    if op1 is None:
        P.op("V", ins, [out],
             lambda e: e.tensor_scalar(out.ap, in_.ap, a1, None, op0))
    else:
        P.op("V", ins, [out],
             lambda e: e.tensor_scalar(out.ap, in_.ap, a1, a2, op0, op1))


def v_stt(P, out, in0, scalar, in1, op0, op1):
    ins = [in0, in1] + ([scalar] if isinstance(scalar, Ref) else [])
    sc = scalar.ap if isinstance(scalar, Ref) else scalar
    P.op("V", ins, [out],
         lambda e: e.scalar_tensor_tensor(out.ap, in0.ap, sc, in1.ap, op0, op1))


def v_red(P, out, in_, alu=ALU.max, negate=False):
    P.op("V", [in_], [out],
         lambda e: e.tensor_reduce(out.ap, in_.ap, AX.X, alu, negate=negate))


def s_act(P, out, in_, func, bias=0.0, scale=1.0):
    ins = [in_] + ([bias] if isinstance(bias, Ref) else [])
    b = bias.ap if isinstance(bias, Ref) else bias
    P.op("S", ins, [out],
         lambda e: e.activation(out.ap, in_.ap, func, bias=b, scale=scale))


def g_memset(P, out, val):
    P.op("G", [], [out], lambda e: e.memset(out.ap, val))


def v_memset(P, out, val):
    P.op("V", [], [out], lambda e: e.memset(out.ap, val))



GROUPS = [[0, 1], [2, 3], [4, 5], [6, 7]]


class Ctx:
    pass


def _load_consts(P, C):
    nc = P.nc
    C.d_ones_f = P.dram("c_ones_f", [128, 128], F32, kind="ExternalInput")
    C.d_eye_f = P.dram("c_eye_f", [128, 128], F32, kind="ExternalInput")
    C.d_bd64 = P.dram("c_bd64", [128, 2], F32, kind="ExternalInput")
    C.d_ones16 = P.dram("c_ones16", [128, 512], BF16, kind="ExternalInput")
    C.d_zeros16 = P.dram("c_zeros16", [128, 512], BF16, kind="ExternalInput")
    C.d_eh = P.dram("c_eh", [8, 8, 128], F32, kind="ExternalInput")
    C.d_cst = P.dram("c_cst", [128, 8], F32, kind="ExternalInput")

    C.ones_f = P.sbuf("s_ones_f", [128, 128], F32)
    C.eye_f = P.sbuf("s_eye_f", [128, 128], F32)
    C.bd64 = P.sbuf("s_bd64", [128, 2], F32)
    C.ones16 = P.sbuf("s_ones16", [128, 512], BF16)
    C.cst = P.sbuf("s_cst", [128, 8], F32)
    P.dma("y", C.ones_f.full, C.d_ones_f.full)
    P.dma("y", C.eye_f.full, C.d_eye_f.full)
    P.dma("y", C.bd64.full, C.d_bd64.full)
    P.dma("y", C.ones16.full, C.d_ones16.full)
    P.dma("y", C.cst.full, C.d_cst.full)


def _bcast_row(P, C, out_f32, row_ref, pb, n):
    pref = Ref(pb, pb.t[0:128, 0:n])
    P.mms(pref, [(Ref(C.ones_f, C.ones_f.t[0:1, 0:128]), row_ref)])
    v_copy(P, out_f32, pref)


def emit_ln(P, C, l):
    st = P.nc.sbuf_tensor
    import contextlib
    ph = contextlib.ExitStack()
    xt2 = [[Buf(ph.enter_context(st(f"ln{l}_x{b}_{i}", [128, 512], F32)), f"ln{l}_x{b}_{i}")
            for i in range(8)] for b in range(2)]
    sq = [Buf(ph.enter_context(st(f"ln{l}_sq{i}", [128, 512], F32)), f"ln{l}_sq{i}") for i in range(8)]
    t1 = Buf(ph.enter_context(st(f"ln{l}_t1", [128, 512], F32)), f"ln{l}_t1")
    t2 = Buf(ph.enter_context(st(f"ln{l}_t2", [128, 512], F32)), f"ln{l}_t2")
    bmu = Buf(ph.enter_context(st(f"ln{l}_bmu", [128, 512], F32)), f"ln{l}_bmu")
    brs = Buf(ph.enter_context(st(f"ln{l}_brs", [128, 512], F32)), f"ln{l}_brs")
    for tg in range(4):
        cols = slice(512 * tg, 512 * tg + 512)
        xt = xt2[tg % 2]
        for c in range(8):
            P.dma("y", xt[c].full, C.x_src[c][:, cols])
        for c in range(8):
            s_act(P, sq[c].full, xt[c].full, AF.Square)
        pbx = C.pb[0] if tg % 2 == 0 else C.pb[2]
        pbq = C.pb[1] if tg % 2 == 0 else C.pb[3]
        P.mms(pbx.full, [(C.ones_f.full, xt[c].full) for c in range(8)])
        P.mms(pbq.full, [(C.ones_f.full, sq[c].full) for c in range(8)])
        v_ts(P, bmu.full, pbx.full, 1.0 / 1024.0, ALU.mult)
        P.dma("y", Ref(C.r_dram, C.r_dram.t[0:1, cols]), Ref(bmu, bmu.t[0:1, :]))
        v_ts(P, t1.full, pbq.full, 1.0 / 1024.0, ALU.mult)
        v_tt(P, t2.full, bmu.full, bmu.full, ALU.mult)
        v_tt(P, t1.full, t1.full, t2.full, ALU.subtract)
        s_act(P, t2.full, t1.full, AF.Ln, bias=Ref(C.cst, C.cst.t[:, 0:1]))
        s_act(P, brs.full, t2.full, AF.Exp, scale=-0.5)
        P.dma("y", Ref(C.r_dram, C.r_dram.t[1:2, cols]), Ref(brs, brs.t[0:1, :]))
        for c in range(8):
            v_tt(P, t2.full, xt[c].full, bmu.full, ALU.subtract)
            v_tt(P, Ref(C.xno[c], C.xno[c].t[:, cols]), t2.full, brs.full, ALU.mult)
            P.dma("y", Ref(C.gin, C.gin.t[128 * c:128 * c + 128, cols]),
                  Ref(C.xno[c], C.xno[c].t[:, cols]))
    P.barrier()
    ph.close()


def _xn32(P, C, dest, c, tg, bmu, brs, tmp):
    cols = slice(512 * tg, 512 * tg + 512)
    P.dma("y", tmp.full, C.x_src[c][:, cols])
    v_tt(P, dest, tmp.full, bmu.full, ALU.subtract)
    v_tt(P, dest, dest, brs.full, ALU.mult)


def _row_bcasts(P, C, bmu, brs, tg):
    cols = slice(512 * tg, 512 * tg + 512)
    P.dma("y", Ref(C.rowscr, C.rowscr.t[0:1, 0:512]), Ref(C.r_dram, C.r_dram.t[0:1, cols]))
    P.dma("y", Ref(C.rowscr, C.rowscr.t[0:1, 512:1024]), Ref(C.r_dram, C.r_dram.t[1:2, cols]))
    _bcast_row(P, C, bmu.full, Ref(C.rowscr, C.rowscr.t[0:1, 0:512]), C.pb[5], 512)
    _bcast_row(P, C, brs.full, Ref(C.rowscr, C.rowscr.t[0:1, 512:1024]), C.pb[5], 512)


class Chunk2:
    def __init__(self, buf, c0, w=128):
        self.buf = buf
        self.c0 = c0
        self.w = w

    @property
    def full(self):
        return Ref(self.buf, self.buf.t[:, self.c0:self.c0 + self.w])


class Chunk:
    def __init__(self, buf, r0, nrows=128):
        self.buf = buf
        self.r0 = r0
        self.nrows = nrows

    def __getitem__(self, idx):
        rows, cols = idx
        r = range(self.nrows)[rows]
        return Ref(self.buf, self.buf.t[self.r0 + r.start:self.r0 + r.stop, cols])


def emit_update(P, C, c, cols, g16_ref, extra_mult=None, k=[0]):
    i = k[0] % 2
    k[0] += 1
    n = cols.stop - cols.start
    xt = Ref(C.u_x[i], C.u_x[i].t[:, 0:n])
    st_ = Ref(C.u_s[i], C.u_s[i].t[:, 0:n])
    P.dma("y", xt, C.x_src[c][:, cols])
    P.dma("y", st_, Ref(C.s_dram, C.s_dram.t[128 * c:128 * c + 128, cols]))
    if extra_mult is not None:
        v_tt(P, st_, st_, extra_mult, ALU.mult)
    v_tt(P, st_, st_, g16_ref, ALU.mult)
    v_tt(P, xt, xt, st_, ALU.add)
    P.dma("y", Ref(C.x_out, C.x_out.t[128 * c:128 * c + 128, cols]), xt)


def emit_gather(P, C):
    P.coll("AllGather", ALU.bypass, GROUPS, C.gin.full, C.gout.full)


def _mkbuf(P, ph, name, shape, dtype):
    t = ph.enter_context(P.nc.sbuf_tensor(name, list(shape), dtype))
    return Buf(t, name)


def emit_attn_mh(P, C, l):
    import contextlib
    ph = contextlib.ExitStack()
    scale = ATTN_SCALE[l]
    has_proj = l in (1, 2)
    qkv_bias = (l == 3)
    v_from_k = (l == 3)

    XN = [_mkbuf(P, ph, f"at{l}_xn{c}", (128, 4096), BF16) for c in range(8)]
    kf = _mkbuf(P, ph, f"at{l}_kf", (128, 4096), BF16)
    vt = [_mkbuf(P, ph, f"at{l}_vt{i}", (128, 130), BF16) for i in range(32)]
    qf = _mkbuf(P, ph, f"at{l}_qf", (128, 2048), BF16)
    et = [_mkbuf(P, ph, f"at{l}_et{i}", (128, 512), BF16) for i in range(3)]
    mk = [_mkbuf(P, ph, f"at{l}_mk{i}", (128, 512), BF16) for i in range(2)]
    wqb = _mkbuf(P, ph, f"at{l}_wq", (128, 1024), BF16)
    wkb = _mkbuf(P, ph, f"at{l}_wk", (128, 1024), BF16)
    wvb = _mkbuf(P, ph, f"at{l}_wv", (128, 1024), BF16)
    wq = [Chunk2(wqb, 128 * i) for i in range(8)]
    wk = [Chunk2(wkb, 128 * i) for i in range(8)]
    wv = [Chunk2(wvb, 128 * i) for i in range(8)]
    msq = [_mkbuf(P, ph, f"at{l}_msq0", (128, 512), F32)] * 2
    mq4 = _mkbuf(P, ph, f"at{l}_mq4", (2, 16), F32)
    mred = _mkbuf(P, ph, f"at{l}_mred", (2, 4), F32)
    negrow = _mkbuf(P, ph, f"at{l}_negrow", (1, 2), F32)
    bm = _mkbuf(P, ph, f"at{l}_bm", (128, 2), F32)
    negln = _mkbuf(P, ph, f"at{l}_negln", (1, 512), F32)
    rb = _mkbuf(P, ph, f"at{l}_rb", (64, 512), F32)
    sst = _mkbuf(P, ph, f"at{l}_sst", (128, 512), F32)
    bkb = _mkbuf(P, ph, f"at{l}_bkb", (128, 128), F32) if v_from_k else None
    att16 = [_mkbuf(P, ph, f"at{l}_a16{i}", (128, 2048), BF16) for i in range(8)] if has_proj else None

    if qkv_bias:
        bq_sb = _mkbuf(P, ph, f"at{l}_bq", (128, 8), F32)
        bk_sb = _mkbuf(P, ph, f"at{l}_bk", (128, 8), F32)
        bkr_sb = _mkbuf(P, ph, f"at{l}_bkr", (1, 1024), F32)
        P.dma("y", bq_sb.full, C.W["w3_bq"].full)
        P.dma("y", bk_sb.full, C.W["w3_bk"].full)
        P.dma("y", bkr_sb.full, C.W["w3_bkr"].full)
    if l == 2:
        bo_sb = _mkbuf(P, ph, "at2_bo", (128, 8), F32)
        P.dma("y", bo_sb.full, C.W["w2_bo"].full)

    for c in range(8):
        P.dma("y", Ref(XN[c], XN[c].t[:, 0:2048]),
              Ref(C.gout, C.gout.t[128 * c:128 * c + 128, :]))
        P.dma("y", Ref(XN[c], XN[c].t[:, 2048:4096]),
              Ref(C.gout, C.gout.t[1024 + 128 * c:1024 + 128 * c + 128, :]))

    if l == 2:
        emit_gf(P, C, ph, XN)

    for kt in range(32):
        v_memset(P, Ref(vt[kt], bass.AP(vt[kt].t, 64, [[130, 128], [65, 2], [1, 1]])), 1.0)

    if l == 1:
        qkvT = C.W["w1_qkvT"]
    elif l == 2:
        qkvT = C.W["w2_qkvT"]

    live = {}
    for qg in range(4):
        live[qg] = [kb for kb in range(32) if tile_live(l, qg, kb)]

    pb_s, pb_av0, pb_av1, pb_b, pb_m, pb_x, pb_o = (C.pb[0], C.pb[1], C.pb[2],
                                                    C.pb[3], C.pb[4], C.pb[5], C.pb[6])
    pb_sr = (C.pb[0], C.pb[7])
    for hp in range(8):
        def _wload(dst, src, col0):
            W = src.t.shape[1]
            P.dma("g", Ref(dst, bass.AP(dst.t, 0, [[1024, 128], [128, 8], [1, 128]])),
                  Ref(src, bass.AP(src.t, col0, [[W, 128], [W * 128, 8], [1, 128]])))
        if l == 3:
            _wload(wqb, C.W["w3_q"], 128 * hp)
            _wload(wkb, C.W["w3_k"], 128 * hp)
        else:
            _wload(wqb, qkvT, 128 * hp)
            _wload(wkb, qkvT, 1024 + 128 * hp)
            _wload(wvb, qkvT, 2048 + 128 * hp)
        wvv = wk if v_from_k else wv
        bi = [0]
        pb_bld = (pb_x, pb_o)

        def pbx():
            bi[0] += 1
            return pb_bld[bi[0] % 2]
        for ktg in range(8):
            cols = slice(512 * ktg, 512 * ktg + 512)
            pbb = pbx()
            P.mms(pbb.full, [(wk[kc].full, Ref(XN[kc], XN[kc].t[:, cols])) for kc in range(8)])
            if qkv_bias:
                v_ts(P, Ref(kf, kf.t[:, cols]), pbb.full,
                     Ref(bk_sb, bk_sb.t[:, hp:hp + 1]), ALU.add)
            else:
                v_copy(P, Ref(kf, kf.t[:, cols]), pbb.full)
        if v_from_k:
            _bcast_row(P, C, Ref(bkb, bkb.t[0:128, 0:128]),
                       Ref(bkr_sb, bkr_sb.t[0:1, 128 * hp:128 * hp + 128]), pb_b, 128)
        for kt in range(32):
            kcols = slice(128 * kt, 128 * kt + 128)
            pbb = pbx()
            P.mms(Ref(pbb, pbb.t[:, 0:128]),
                  [(Ref(XN[kc], XN[kc].t[:, kcols]), wvv[kc].full) for kc in range(8)])
            src = Ref(pbb, bass.AP(pbb.t, 0, [[512, 128], [64, 2], [1, 64]]))
            dst = Ref(vt[kt], bass.AP(vt[kt].t, 0, [[130, 128], [65, 2], [1, 64]]))
            if v_from_k:
                bsrc = Ref(bkb, bass.AP(bkb.t, 0, [[128, 128], [64, 2], [1, 64]]))
                v_stt(P, dst, src, 1.0, bsrc, ALU.mult, ALU.add)
            else:
                v_copy(P, dst, src)
        for qtg in range(4):
            cols = slice(512 * qtg, 512 * qtg + 512)
            pbb = pbx()
            P.mms(pbb.full, [(wq[kc].full, Ref(C.xno[kc], C.xno[kc].t[:, cols])) for kc in range(8)])
            if qkv_bias:
                v_ts(P, Ref(qf, qf.t[:, cols]), pbb.full,
                     Ref(bq_sb, bq_sb.t[:, hp:hp + 1]), ALU.add)
            else:
                v_copy(P, Ref(qf, qf.t[:, cols]), pbb.full)
        for qtg in range(4):
            cols = slice(512 * qtg, 512 * qtg + 512)
            s_act(P, msq[0].full, Ref(qf, qf.t[:, cols]), AF.Square)
            P.mms(Ref(pb_m, pb_m.t[0:2, 0:512]), [(C.bd64.full, msq[0].full)])
            v_red(P, Ref(mq4, mq4.t[0:2, qtg:qtg + 1]), Ref(pb_m, pb_m.t[0:2, 0:512]))
        v_red(P, Ref(mred, mred.t[0:2, 0:1]), Ref(mq4, mq4.t[0:2, 0:4]))
        for ktg in range(8):
            cols = slice(512 * ktg, 512 * ktg + 512)
            s_act(P, msq[1].full, Ref(kf, kf.t[:, cols]), AF.Square)
            P.mms(Ref(pb_m, pb_m.t[0:2, 0:512]), [(C.bd64.full, msq[1].full)])
            v_red(P, Ref(mq4, mq4.t[0:2, 8 + ktg:9 + ktg]), Ref(pb_m, pb_m.t[0:2, 0:512]))
        v_red(P, Ref(mred, mred.t[0:2, 1:2]), Ref(mq4, mq4.t[0:2, 8:16]))
        v_tt(P, Ref(mred, mred.t[0:2, 2:3]), Ref(mred, mred.t[0:2, 0:1]),
             Ref(mred, mred.t[0:2, 1:2]), ALU.mult)
        s_act(P, Ref(mred, mred.t[0:2, 3:4]), Ref(mred, mred.t[0:2, 2:3]),
              AF.Sqrt, bias=Ref(C.cst, C.cst.t[0:2, 1:2]))
        v_ts(P, Ref(mred, mred.t[0:2, 0:1]), Ref(mred, mred.t[0:2, 3:4]),
             -scale, ALU.mult)
        P.transpose(Ref(pb_m, pb_m.t[0:1, 0:2]), Ref(mred, mred.t[0:2, 0:1]),
                    Ref(C.eye_f, C.eye_f.t[0:2, 0:2]))
        v_copy(P, negrow.full, Ref(pb_m, pb_m.t[0:1, 0:2]))
        for h in range(2):
            P.mms(Ref(pb_b, pb_b.t[0:128, 0:1]),
                  [(Ref(C.ones_f, C.ones_f.t[0:1, 0:128]),
                    Ref(negrow, negrow.t[0:1, h:h + 1]))])
            v_copy(P, Ref(bm, bm.t[:, h:h + 1]), Ref(pb_b, pb_b.t[0:128, 0:1]))

        for qg in range(4):
            qcols = slice(512 * qg, 512 * qg + 512)
            kbs = live[qg]
            pavs = (pb_av0, pb_av1)
            steps = [(ikb, kb, h) for ikb, kb in enumerate(kbs) for h in range(2)]
            pend = None
            for si, (ikb, kb, h) in enumerate(steps):
                need_mask = tile_needs_mask(l, qg, kb)
                if need_mask and h == 0:
                    mkr = mk[ikb % 2]
                    tab = C.mtab[kb // 16]
                    O = MC + 256 * (4 * qg - (kb % 16))
                    P.dma("y", mkr.full,
                          Ref(tab, bass.AP(tab.t, O, [[MW, 128], [256, 4], [1, 128]])))
                etr = et[si % 3]
                psb = pb_sr[si % 2]
                P.mms(psb.full,
                      [(Ref(kf, kf.t[64 * h:64 * h + 64, 128 * kb:128 * kb + 128]),
                        Ref(qf, qf.t[64 * h:64 * h + 64, qcols]))])
                s_act(P, etr.full, psb.full, AF.Exp,
                      bias=Ref(bm, bm.t[:, h:h + 1]), scale=scale)
                if need_mask:
                    v_tt(P, etr.full, etr.full, mk[ikb % 2].full, ALU.mult)
                if pend is not None:
                    (pikb, ph_, petr) = pend
                    P.mms(Ref(pavs[ph_], pavs[ph_].t[0:65, :]),
                          [(Ref(vt[steps[si - 1][1]], vt[steps[si - 1][1]].t[:, 65 * ph_:65 * ph_ + 65]), petr)],
                          start=(pikb == 0), stop=(pikb == len(kbs) - 1))
                pend = (ikb, h, etr.full)
            if pend is not None:
                (pikb, ph_, petr) = pend
                P.mms(Ref(pavs[ph_], pavs[ph_].t[0:65, :]),
                      [(Ref(vt[kbs[-1]], vt[kbs[-1]].t[:, 65 * ph_:65 * ph_ + 65]), petr)],
                      start=(pikb == 0), stop=(pikb == len(kbs) - 1))
            for h in range(2):
                H = 2 * hp + h
                pav = pavs[h]
                s_act(P, negln.full, Ref(pav, pav.t[64:65, :]), AF.Ln)
                P.mms(Ref(pb_o, pb_o.t[0:64, :]),
                      [(Ref(C.ones_f, C.ones_f.t[0:1, 0:64]), negln.full)])
                s_act(P, rb.full, Ref(pb_o, pb_o.t[0:64, :]), AF.Exp, scale=-1.0)
                if has_proj:
                    dest = Ref(att16[hp], att16[hp].t[64 * h:64 * h + 64, qcols])
                    v_tt(P, dest, Ref(pav, pav.t[0:64, :]), rb.full, ALU.mult)
                else:
                    dst = Ref(sst, sst.t[0:64, :])
                    v_tt(P, dst, Ref(pav, pav.t[0:64, :]), rb.full, ALU.mult)
                    P.dma("y", Ref(C.s_dram, C.s_dram.t[64 * H:64 * H + 64, qcols]), dst)

    if has_proj:
        woT = C.W["w1_woT"] if l == 1 else C.W["w2_woT"]
        wo = [_mkbuf(P, ph, f"at{l}_wo{i}", (128, 128), BF16) for i in range(8)]
        for mc in range(8):
            for kc in range(8):
                P.dma("g", wo[kc].full,
                      Ref(woT, woT.t[128 * kc:128 * kc + 128, 128 * mc:128 * mc + 128]))
            for tg in range(4):
                cols = slice(512 * tg, 512 * tg + 512)
                P.mms(pb_o.full,
                      [(wo[kc].full, Ref(att16[kc], att16[kc].t[:, cols])) for kc in range(8)])
                if l == 2:
                    v_ts(P, sst.full, pb_o.full, Ref(bo_sb, bo_sb.t[:, mc:mc + 1]), ALU.add)
                else:
                    v_copy(P, sst.full, pb_o.full)
                P.dma("y", Ref(C.s_dram, C.s_dram.t[128 * mc:128 * mc + 128, cols]), sst.full)
    P.barrier()
    ph.close()


def emit_gf(P, C, ph, XN):
    gcs = _mkbuf(P, ph, "gf_gcs", (128, 8), F32)
    gc16 = _mkbuf(P, ph, "gf_gc16", (128, 8), BF16)
    h1 = [_mkbuf(P, ph, f"gf_h1{i}", (128, 1), BF16) for i in range(4)]
    h2 = [_mkbuf(P, ph, f"gf_h2{i}", (128, 1), BF16) for i in range(2)]
    gw1 = [_mkbuf(P, ph, f"gf_w1{i}", (128, 512), BF16) for i in range(8)]
    gw2 = [_mkbuf(P, ph, f"gf_w2{i}", (128, 256), BF16) for i in range(4)]
    gw3 = [_mkbuf(P, ph, f"gf_w3{i}", (128, 1), BF16) for i in range(2)]
    gb1 = _mkbuf(P, ph, "gf_gb1", (128, 4), F32)
    gb2 = _mkbuf(P, ph, "gf_gb2", (128, 2), F32)
    gb3 = _mkbuf(P, ph, "gf_gb3", (1, 1), F32)
    for kc in range(8):
        P.dma("g", gw1[kc].full, Ref(C.W["w2_gw1"], C.W["w2_gw1"].t[128 * kc:128 * kc + 128, :]))
    for kc in range(4):
        P.dma("g", gw2[kc].full, Ref(C.W["w2_gw2"], C.W["w2_gw2"].t[128 * kc:128 * kc + 128, :]))
    for kc in range(2):
        P.dma("g", gw3[kc].full, Ref(C.W["w2_gw3"], C.W["w2_gw3"].t[128 * kc:128 * kc + 128, :]))
    P.dma("y", gb1.full, C.W["w2_gb1"].full)
    P.dma("y", gb2.full, C.W["w2_gb2"].full)
    P.dma("y", gb3.full, C.W["w2_gb3"].full)
    for c in range(8):
        v_red(P, Ref(gcs, gcs.t[:, c:c + 1]), Ref(XN[c], XN[c].t[:, 0:4096]), ALU.add)
    v_copy(P, gc16.full, gcs.full)
    pb = C.pb[7]
    for mc in range(4):
        P.mms(Ref(pb, pb.t[0:128, 0:1]),
              [(Ref(gw1[kc], gw1[kc].t[:, 128 * mc:128 * mc + 128]),
                Ref(gc16, gc16.t[:, kc:kc + 1])) for kc in range(8)])
        s_act(P, h1[mc].full, Ref(pb, pb.t[0:128, 0:1]), AF.Gelu,
              bias=Ref(gb1, gb1.t[:, mc:mc + 1]), scale=1.0 / 4096.0)
    for mc in range(2):
        P.mms(Ref(pb, pb.t[0:128, 0:1]),
              [(Ref(gw2[kc], gw2[kc].t[:, 128 * mc:128 * mc + 128]), h1[kc].full)
               for kc in range(4)])
        s_act(P, h2[mc].full, Ref(pb, pb.t[0:128, 0:1]), AF.Gelu,
              bias=Ref(gb2, gb2.t[:, mc:mc + 1]))
    P.mms(Ref(pb, pb.t[0:1, 0:1]),
          [(Ref(gw3[kc], gw3[kc].t[:, 0:1]), h2[kc].full) for kc in range(2)])
    s_act(P, C.gf.full, Ref(pb, pb.t[0:1, 0:1]), AF.Sigmoid,
          bias=Ref(gb3, gb3.t[0:1, 0:1]))


def emit_attn_l0(P, C):
    import contextlib
    ph = contextlib.ExitStack()
    scale = ATTN_SCALE[0]
    XN = [_mkbuf(P, ph, f"a0_xn{c}", (128, 4096), BF16) for c in range(8)]
    qfc = [_mkbuf(P, ph, f"a0_q{c}", (128, 512), BF16) for c in range(8)]
    qst = [_mkbuf(P, ph, f"a0_qst{i}", (128, 512), BF16) for i in range(2)]
    kfc = [_mkbuf(P, ph, f"a0_k{c}", (128, 512), BF16) for c in range(8)]
    vtc = [_mkbuf(P, ph, f"a0_v{i}", (128, 1024), BF16) for i in range(4)]
    wk16 = [_mkbuf(P, ph, f"a0_wk{i}", (128, 1024), BF16) for i in range(8)]
    wv16 = [_mkbuf(P, ph, f"a0_wvf{i}", (128, 1024), BF16) for i in range(8)]
    wq16 = wk16
    et = [_mkbuf(P, ph, f"a0_et{i}", (128, 512), BF16) for i in range(4)]
    mk = _mkbuf(P, ph, "a0_mk", (128, 512), BF16)
    bq_sb = _mkbuf(P, ph, "a0_bq", (128, 8), F32)
    bk_sb = _mkbuf(P, ph, "a0_bk", (128, 8), F32)
    bvr_sb = _mkbuf(P, ph, "a0_bvr", (1, 1024), F32)
    mrow = _mkbuf(P, ph, "a0_mrow", (1, 32), F32)
    bmcol = _mkbuf(P, ph, "a0_bm", (128, 1), F32)
    den1 = _mkbuf(P, ph, "a0_den1", (1, 512), F32)
    negln = _mkbuf(P, ph, "a0_negln", (1, 512), F32)
    av2 = [_mkbuf(P, ph, f"a0_av{i}", (128, 512), F32) for i in range(2)]
    bvb = _mkbuf(P, ph, "a0_bvb", (128, 512), F32)
    wsq = _mkbuf(P, ph, "a0_wsq", (128, 512), F32)

    P.dma("y", bq_sb.full, C.W["w0_bq"].full)
    P.dma("y", bk_sb.full, C.W["w0_bk"].full)
    P.dma("y", bvr_sb.full, C.W["w0_bvr"].full)
    for c in range(8):
        P.dma("y", Ref(XN[c], XN[c].t[:, 0:2048]),
              Ref(C.gout, C.gout.t[128 * c:128 * c + 128, :]))
        P.dma("y", Ref(XN[c], XN[c].t[:, 2048:4096]),
              Ref(C.gout, C.gout.t[1024 + 128 * c:1024 + 128 * c + 128, :]))

    pb_s, pb_a0, pb_a1, pb_a2, pb_a3, pb_d, pb_x, pb_b = C.pb
    pb_a = [pb_a0, pb_a1, pb_a2, pb_a3]
    onescol_f = Ref(C.ones_f, C.ones_f.t[:, 0:1])
    onesrow_f = Ref(C.ones_f, C.ones_f.t[0:1, 0:128])
    ones16col = Ref(C.ones16, C.ones16.t[:, 0:1])

    for kc in range(8):
        P.dma("g", wq16[kc].full, Ref(C.W["w0_qT"], C.W["w0_qT"].t[128 * kc:128 * kc + 128, :]))
        P.dma("g", wv16[kc].full, Ref(C.W["w0_vT"], C.W["w0_vT"].t[128 * kc:128 * kc + 128, :]))
    for qtg in range(4):
        cols = slice(512 * qtg, 512 * qtg + 512)
        for mc in range(8):
            P.mms(pb_x.full, [(Ref(wq16[kc], wq16[kc].t[:, 128 * mc:128 * mc + 128]),
                               Ref(C.xno[kc], C.xno[kc].t[:, cols])) for kc in range(8)])
            qs = qst[mc % 2]
            v_ts(P, qs.full, pb_x.full, Ref(bq_sb, bq_sb.t[:, mc:mc + 1]), ALU.add)
            P.dma("y", Ref(C.q_dram, C.q_dram.t[128 * mc:128 * mc + 128, cols]), qs.full)
            s_act(P, wsq.full, qs.full, AF.Square)
            P.mms(Ref(pb_d, pb_d.t[0:1, 0:512]), [(onescol_f, wsq.full)],
                  start=(mc == 0), stop=(mc == 7))
        v_red(P, Ref(mrow, mrow.t[0:1, qtg:qtg + 1]), Ref(pb_d, pb_d.t[0:1, 0:512]))
    v_red(P, Ref(mrow, mrow.t[0:1, 8:9]), Ref(mrow, mrow.t[0:1, 0:4]))
    for ktg in range(8):
        cols = slice(512 * ktg, 512 * ktg + 512)
        for c in range(8):
            s_act(P, wsq.full, Ref(XN[c], XN[c].t[:, cols]), AF.Square)
            P.mms(Ref(pb_d, pb_d.t[0:1, 0:512]), [(onescol_f, wsq.full)],
                  start=(c == 0), stop=(c == 7))
        v_red(P, Ref(mrow, mrow.t[0:1, 16 + ktg:17 + ktg]), Ref(pb_d, pb_d.t[0:1, 0:512]))
    v_red(P, Ref(mrow, mrow.t[0:1, 9:10]), Ref(mrow, mrow.t[0:1, 16:24]))
    for kc in range(8):
        P.dma("g", wk16[kc].full, Ref(C.W["w0_kT"], C.W["w0_kT"].t[128 * kc:128 * kc + 128, :]))
    for kc in range(8):
        for hv in range(2):
            s_act(P, Ref(wsq, wsq.t[:, 0:512]),
                  Ref(wk16[kc], wk16[kc].t[:, 512 * hv:512 * hv + 512]), AF.Square)
            P.mms(Ref(pb_d, pb_d.t[0:1, 0:512]), [(onescol_f, Ref(wsq, wsq.t[:, 0:512]))],
                  start=(kc == 0 and hv == 0), stop=(kc == 7 and hv == 1))
    v_red(P, Ref(mrow, mrow.t[0:1, 10:11]), Ref(pb_d, pb_d.t[0:1, 0:512]), alu=ALU.add)
    s_act(P, Ref(wsq, wsq.t[0:128, 0:8]), bk_sb.full, AF.Square)
    P.mms(Ref(pb_d, pb_d.t[0:1, 0:8]), [(onescol_f, Ref(wsq, wsq.t[:, 0:8]))])
    v_red(P, Ref(mrow, mrow.t[0:1, 11:12]), Ref(pb_d, pb_d.t[0:1, 0:8]), alu=ALU.add)
    z2 = Ref(C.cst, C.cst.t[0:1, 1:2])
    v_tt(P, Ref(mrow, mrow.t[0:1, 12:13]), Ref(mrow, mrow.t[0:1, 10:11]),
         Ref(mrow, mrow.t[0:1, 9:10]), ALU.mult)
    s_act(P, Ref(mrow, mrow.t[0:1, 13:14]), Ref(mrow, mrow.t[0:1, 12:13]), AF.Sqrt, bias=z2)
    s_act(P, Ref(mrow, mrow.t[0:1, 14:15]), Ref(mrow, mrow.t[0:1, 11:12]), AF.Sqrt, bias=z2)
    v_tt(P, Ref(mrow, mrow.t[0:1, 13:14]), Ref(mrow, mrow.t[0:1, 13:14]),
         Ref(mrow, mrow.t[0:1, 14:15]), ALU.add)
    s_act(P, Ref(mrow, mrow.t[0:1, 15:16]), Ref(mrow, mrow.t[0:1, 8:9]), AF.Sqrt, bias=z2)
    v_tt(P, Ref(mrow, mrow.t[0:1, 15:16]), Ref(mrow, mrow.t[0:1, 15:16]),
         Ref(mrow, mrow.t[0:1, 13:14]), ALU.mult)
    v_ts(P, Ref(mrow, mrow.t[0:1, 15:16]), Ref(mrow, mrow.t[0:1, 15:16]), -scale, ALU.mult)
    P.mms(Ref(pb_b, pb_b.t[0:128, 0:1]), [(onesrow_f, Ref(mrow, mrow.t[0:1, 15:16]))])
    v_copy(P, bmcol.full, Ref(pb_b, pb_b.t[0:128, 0:1]))

    for kq in range(8):
        cols = slice(512 * kq, 512 * kq + 512)
        for mc in range(8):
            P.mms(pb_x.full, [(Ref(wk16[kc], wk16[kc].t[:, 128 * mc:128 * mc + 128]),
                               Ref(XN[kc], XN[kc].t[:, cols])) for kc in range(8)])
            v_ts(P, kfc[mc].full, pb_x.full,
                 Ref(bk_sb, bk_sb.t[:, mc:mc + 1]), ALU.add)
        for vc in range(2):
            P.mms(Ref(pb_b, pb_b.t[0:128, 0:512]),
                  [(onesrow_f, Ref(bvr_sb, bvr_sb.t[0:1, 512 * vc:512 * vc + 512]))])
            v_copy(P, bvb.full, Ref(pb_b, pb_b.t[0:128, 0:512]))
            for kt in range(4):
                kcols = slice(512 * kq + 128 * kt, 512 * kq + 128 * kt + 128)
                P.mms(Ref(pb_x, pb_x.t[:, 0:512]),
                      [(Ref(XN[kc], XN[kc].t[:, kcols]),
                        Ref(wv16[kc], wv16[kc].t[:, 512 * vc:512 * vc + 512])) for kc in range(8)])
                v_stt(P, Ref(vtc[kt], vtc[kt].t[:, 512 * vc:512 * vc + 512]),
                      Ref(pb_x, pb_x.t[:, 0:512]), 1.0, bvb.full, ALU.mult, ALU.add)
        for qg in range(4):
            qcols = slice(512 * qg, 512 * qg + 512)
            for kc in range(8):
                P.dma("y", qfc[kc].full,
                      Ref(C.q_dram, C.q_dram.t[128 * kc:128 * kc + 128, qcols]))
            pend = None
            for kt in range(4):
                kb = 4 * kq + kt
                psb = pb_s if kt % 2 == 0 else pb_x
                P.mms(psb.full,
                      [(Ref(kfc[kc], kfc[kc].t[:, 128 * kt:128 * kt + 128]),
                        qfc[kc].full) for kc in range(8)])
                s_act(P, et[kt].full, psb.full, AF.Exp, bias=bmcol.full, scale=scale)
                if tile_needs_mask(0, qg, kb):
                    tab = C.mtab[kb // 16]
                    O = MC + 256 * (4 * qg - (kb % 16))
                    P.dma("y", mk.full,
                          Ref(tab, bass.AP(tab.t, O, [[MW, 128], [256, 4], [1, 128]])))
                    v_tt(P, et[kt].full, et[kt].full, mk.full, ALU.mult)
                if pend is not None:
                    P.mms(Ref(pb_d, pb_d.t[0:1, 0:512]), [(ones16col, et[pend].full)],
                          start=(pend == 0), stop=False)
                pend = kt
            P.mms(Ref(pb_d, pb_d.t[0:1, 0:512]), [(ones16col, et[pend].full)],
                  start=(pend == 0), stop=True)
            v_copy(P, den1.full, Ref(pb_d, pb_d.t[0:1, 0:512]))
            P.dma("g", Ref(C.s_dram, C.s_dram.t[1024 + qg:1025 + qg, qcols]), den1.full,
                  accum_op=(None if kq == 0 else ALU.add))
            for dcp in range(2):
                for mcl in range(4):
                    mc = 4 * dcp + mcl
                    P.mms(Ref(pb_a[mcl], pb_a[mcl].t[:, 0:512]),
                          [(Ref(vtc[kt], vtc[kt].t[:, 128 * mc:128 * mc + 128]), et[kt].full)
                           for kt in range(4)])
                    r = av2[mcl % 2]
                    v_copy(P, r.full, Ref(pb_a[mcl], pb_a[mcl].t[:, 0:512]))
                    P.dma("g", Ref(C.s_dram, C.s_dram.t[128 * mc:128 * mc + 128, qcols]),
                          r.full, accum_op=(None if kq == 0 else ALU.add))
    for qg in range(4):
        qcols = slice(512 * qg, 512 * qg + 512)
        P.dma("y", den1.full, Ref(C.s_dram, C.s_dram.t[1024 + qg:1025 + qg, qcols]))
        s_act(P, negln.full, den1.full, AF.Ln)
        P.mms(Ref(pb_b, pb_b.t[0:128, 0:512]), [(onesrow_f, negln.full)])
        s_act(P, C.recip_l0[qg].full, Ref(pb_b, pb_b.t[0:128, 0:512]), AF.Exp, scale=-1.0)
    P.barrier()
    ph.close()


def emit_moe(P, C, l):
    import contextlib
    ph = contextlib.ExitStack()
    experts = [0] if l == 0 else list(range(8))
    w1d = C.W["w0_w1"] if l == 0 else C.W["w3_w1"]
    w2d = C.W["w0_w2"] if l == 0 else C.W["w3_w2"]

    nset = 2 if l == 3 else 1
    w1t = [[_mkbuf(P, ph, f"mo{l}_w1_{b}_{i}", (128, 1024), BF16) for i in range(8)]
           for b in range(nset)]
    w2t = [[_mkbuf(P, ph, f"mo{l}_w2_0_{i}", (128, 1024), BF16) for i in range(8)]]
    ht = [_mkbuf(P, ph, f"mo{l}_h{i}", (128, 512), BF16) for i in range(8)]
    acc = [_mkbuf(P, ph, f"mo{l}_acc{i}", (128, 2048), F32) for i in range(8)]
    b1_sb = _mkbuf(P, ph, f"mo{l}_b1", (128, 32), F32)
    pb_hr = (C.pb[0], C.pb[7])
    pb_o = [C.pb[1], C.pb[2], C.pb[3], C.pb[4]]

    if l == 0:
        b2r16 = _mkbuf(P, ph, "mo0_b2r", (1, 1024), BF16)
        zeros16 = _mkbuf(P, ph, "mo0_z16", (128, 512), BF16)
        P.dma("y", zeros16.full, C.d_zeros16.full)
        P.dma("g", b2r16.full, C.W["w0_b2r"].full)
        P.dma("y", b1_sb.full, C.W["w0_b1"].full)
    else:
        b216 = _mkbuf(P, ph, "mo3_b2", (8, 1024), BF16)
        P.dma("g", b216.full, C.W["w3_b2"].full)
        eh = _mkbuf(P, ph, "mo3_eh", (8, 128), F32)
        G = _mkbuf(P, ph, "mo3_G", (8, 2048), F32)
        G16 = _mkbuf(P, ph, "mo3_G16", (8, 2048), BF16)
        gb = [_mkbuf(P, ph, f"mo3_gb{i}", (128, 512), F32) for i in range(2)]
        _emit_gates(P, C, ph, G, G16)

    hctr = [0]
    for e in experts:
        if l == 3:
            P.dma("y", b1_sb.full, Ref(C.W["w3_b1"], C.W["w3_b1"].t[128 * e:128 * e + 128, :]))
            P.dma("y", eh.full, C.d_eh.ap(1024 * e, [[128, 8], [1, 128]]))
        for fq in range(4):
            bi = fq % nset
            for kc in range(8):
                P.dma("g", w1t[bi][kc].full,
                      Ref(w1d, w1d.t[1024 * e + 128 * kc:1024 * e + 128 * kc + 128,
                                     1024 * fq:1024 * fq + 1024]))
            for fc in range(8):
                r0 = 4096 * e + 1024 * fq + 128 * fc
                P.dma("g", w2t[0][fc].full, Ref(w2d, w2d.t[r0:r0 + 128, :]))
            first_group = (e == experts[0] and fq == 0)
            for tg in range(4):
                cols = slice(512 * tg, 512 * tg + 512)
                if l == 3:
                    gbt = gb[tg % 2]
                    P.mms(Ref(C.pb[5], C.pb[5].t[0:128, 0:512]),
                          [(eh.full, Ref(G, G.t[0:8, cols]))])
                    v_copy(P, gbt.full, Ref(C.pb[5], C.pb[5].t[0:128, 0:512]))
                for fc in range(8):
                    Fg = 8 * fq + fc
                    pbh = pb_hr[hctr[0] % 2]
                    hctr[0] += 1
                    P.mms(pbh.full,
                          [(Ref(w1t[bi][kc], w1t[bi][kc].t[:, 128 * fc:128 * fc + 128]),
                            Ref(C.xno[kc], C.xno[kc].t[:, cols])) for kc in range(8)])
                    if l == 0:
                        v_stt(P, ht[fc].full, pbh.full,
                              Ref(b1_sb, b1_sb.t[:, Fg:Fg + 1]),
                              Ref(zeros16, zeros16.t[:, 0:512]), ALU.add, ALU.max)
                    else:
                        s_act(P, ht[fc].full, pbh.full, AF.Gelu,
                              bias=Ref(b1_sb, b1_sb.t[:, Fg:Fg + 1]))
                        v_tt(P, ht[fc].full, ht[fc].full, gbt.full, ALU.mult)
                for dcp in range(2):
                    for dcl in range(4):
                        dc = 4 * dcp + dcl
                        items = []
                        if first_group:
                            if l == 0:
                                items.append((Ref(b2r16, b2r16.t[0:1, 128 * dc:128 * dc + 128]),
                                              Ref(C.ones16, C.ones16.t[0:1, 0:512])))
                            else:
                                items.append((Ref(b216, b216.t[0:8, 128 * dc:128 * dc + 128]),
                                              Ref(G16, G16.t[0:8, cols])))
                        items += [(Ref(w2t[0][fc], w2t[0][fc].t[:, 128 * dc:128 * dc + 128]),
                                   ht[fc].full) for fc in range(8)]
                        P.mms(pb_o[dcl].full, items)
                        ar = Ref(acc[dc], acc[dc].t[:, cols])
                        if first_group:
                            v_copy(P, ar, pb_o[dcl].full)
                        else:
                            v_tt(P, ar, ar, pb_o[dcl].full, ALU.add)

    if l == 3:
        g16 = [_mkbuf(P, ph, f"mo3_g16{i}", (128, 512), BF16) for i in range(2)]
        for tg in range(4):
            cols = slice(512 * tg, 512 * tg + 512)
            for c in range(8):
                gg = g16[c % 2]
                s_act(P, gg.full, Ref(acc[c], acc[c].t[:, cols]), AF.Silu)
                emit_update(P, C, c, cols, gg.full)
    else:
        x0t = [_mkbuf(P, ph, f"mo0_x0{i}", (128, 512), F32) for i in range(8)]
        sqt = [_mkbuf(P, ph, f"mo0_sq{i}", (128, 512), F32) for i in range(8)]
        tt1 = _mkbuf(P, ph, "mo0_t1", (128, 512), F32)
        tt2 = _mkbuf(P, ph, "mo0_t2", (128, 512), F32)
        bmu = _mkbuf(P, ph, "mo0_bmu", (128, 512), F32)
        brs = _mkbuf(P, ph, "mo0_brs", (128, 512), F32)
        g16 = _mkbuf(P, ph, "mo0_g16", (128, 512), BF16)
        gam_sb = _mkbuf(P, ph, "mo0_gam", (128, 8), F32)
        bet_sb = _mkbuf(P, ph, "mo0_bet", (128, 8), F32)
        P.dma("y", gam_sb.full, C.W["w0_gam"].full)
        P.dma("y", bet_sb.full, C.W["w0_bet"].full)
        for tg in range(4):
            cols = slice(512 * tg, 512 * tg + 512)
            _row_bcasts(P, C, bmu, brs, tg)
            for c in range(8):
                P.dma("y", tt1.full, C.x_src[c][:, cols])
                v_tt(P, tt2.full, tt1.full, bmu.full, ALU.subtract)
                v_tt(P, tt2.full, tt2.full, brs.full, ALU.mult)
                v_tt(P, x0t[c].full, Ref(acc[c], acc[c].t[:, cols]), tt2.full, ALU.add)
            for c in range(8):
                s_act(P, sqt[c].full, x0t[c].full, AF.Square)
            P.mms(C.pb[5].full, [(C.ones_f.full, x0t[c].full) for c in range(8)])
            P.mms(C.pb[6].full, [(C.ones_f.full, sqt[c].full) for c in range(8)])
            v_ts(P, bmu.full, C.pb[5].full, 1.0 / 1024.0, ALU.mult)
            v_ts(P, tt1.full, C.pb[6].full, 1.0 / 1024.0, ALU.mult)
            v_tt(P, tt2.full, bmu.full, bmu.full, ALU.mult)
            v_tt(P, tt1.full, tt1.full, tt2.full, ALU.subtract)
            s_act(P, tt2.full, tt1.full, AF.Ln, bias=Ref(C.cst, C.cst.t[:, 0:1]))
            s_act(P, brs.full, tt2.full, AF.Exp, scale=-0.5)
            for c in range(8):
                v_tt(P, tt1.full, x0t[c].full, bmu.full, ALU.subtract)
                v_tt(P, tt1.full, tt1.full, brs.full, ALU.mult)
                v_ts(P, tt2.full, tt1.full, Ref(gam_sb, gam_sb.t[:, c:c + 1]), ALU.mult,
                     Ref(bet_sb, bet_sb.t[:, c:c + 1]), ALU.add)
                s_act(P, g16.full, tt2.full, AF.Silu)
                emit_update(P, C, c, cols, g16.full,
                            extra_mult=C.recip_l0[tg].full)
    P.barrier()
    ph.close()


def _emit_gates(P, C, ph, G, G16):
    rw16 = _mkbuf(P, ph, "g3_rw", (128, 64), BF16)
    rbr = _mkbuf(P, ph, "g3_rbr", (1, 8), F32)
    rbB = _mkbuf(P, ph, "g3_rbB", (128, 8), F32)
    lg = _mkbuf(P, ph, "g3_lg", (128, 8), F32)
    mskt = _mkbuf(P, ph, "g3_msk", (128, 8), F32)
    eq1 = _mkbuf(P, ph, "g3_eq1", (128, 8), F32)
    eq2 = _mkbuf(P, ph, "g3_eq2", (128, 8), F32)
    gt = _mkbuf(P, ph, "g3_gt", (128, 8), F32)
    colv = _mkbuf(P, ph, "g3_col", (128, 8), F32)
    P.dma("g", rw16.full, C.W["w3_rw"].full)
    P.dma("y", rbr.full, C.W["w3_rbr"].full)
    _bcast_row(P, C, Ref(rbB, rbB.t[0:128, 0:8]), rbr.full, C.pb[6], 8)
    m1 = Ref(colv, colv.t[:, 0:1])
    nm1 = Ref(colv, colv.t[:, 1:2])
    m2 = Ref(colv, colv.t[:, 2:3])
    ex = Ref(colv, colv.t[:, 3:4])
    dn = Ref(colv, colv.t[:, 4:5])
    g1 = Ref(colv, colv.t[:, 5:6])
    g2 = Ref(colv, colv.t[:, 6:7])
    for i in range(16):
        tcols = slice(128 * i, 128 * i + 128)
        P.mms(Ref(C.pb[5], C.pb[5].t[0:128, 0:8]),
              [(Ref(C.xno[kc], C.xno[kc].t[:, tcols]),
                Ref(rw16, rw16.t[:, 8 * kc:8 * kc + 8])) for kc in range(8)])
        v_tt(P, lg.full, Ref(C.pb[5], C.pb[5].t[0:128, 0:8]), rbB.full, ALU.add)
        v_red(P, m1, lg.full)
        v_ts(P, nm1, m1, -1.0, ALU.mult)
        v_ts(P, eq1.full, lg.full, m1, ALU.is_equal)
        v_stt(P, mskt.full, eq1.full, -1e30, lg.full, ALU.mult, ALU.add)
        v_red(P, m2, mskt.full)
        s_act(P, ex, m2, AF.Exp, bias=nm1)
        v_ts(P, dn, ex, 1.0, ALU.add)
        P.op("V", [dn], [g1], lambda e, g1=g1, dn=dn: e.reciprocal(g1.ap, dn.ap))
        v_tt(P, g2, ex, g1, ALU.mult)
        v_ts(P, eq2.full, mskt.full, m2, ALU.is_equal)
        v_ts(P, eq1.full, eq1.full, g1, ALU.mult)
        v_ts(P, eq2.full, eq2.full, g2, ALU.mult)
        v_tt(P, gt.full, eq1.full, eq2.full, ALU.add)
        P.transpose(Ref(C.pb[6], C.pb[6].t[0:8, 0:128]), gt.full, C.eye_f.full)
        v_copy(P, Ref(G, G.t[0:8, tcols]), Ref(C.pb[6], C.pb[6].t[0:8, 0:128]))
    v_copy(P, G16.full, G.full)


def emit_ffn_l1(P, C):
    import contextlib
    ph = contextlib.ExitStack()
    rm16 = [_mkbuf(P, ph, f"f1_rm{i}", (128, 128), BF16) for i in range(32)]
    rup16 = [_mkbuf(P, ph, f"f1_ru{i}", (128, 128), BF16) for i in range(128)]
    hact = [_mkbuf(P, ph, f"f1_h{i}", (128, 256), BF16) for i in range(128)]
    y16 = [_mkbuf(P, ph, f"f1_y{i}", (128, 256), BF16) for i in range(2)]
    cpm = _mkbuf(P, ph, "f1_cpm", (128, 16384), BF16)
    g16 = _mkbuf(P, ph, "f1_g16", (128, 256), BF16)
    for t in range(32):
        P.dma("g", rm16[t].full, Ref(C.W["w1_rm"], C.W["w1_rm"].t[128 * t:128 * t + 128, :]))
    for t in range(128):
        P.dma("g", rup16[t].full, Ref(C.W["w1_rup"], C.W["w1_rup"].t[128 * t:128 * t + 128, :]))
    pb2 = C.pb[2]
    pby = (C.pb[0], C.pb[4])
    pbz = (C.pb[1], C.pb[3])
    zc = [0]
    for tg in range(8):
        cols = slice(256 * tg, 256 * tg + 256)
        for h in range(4):
            for rg in range(8):
                yr = y16[rg % 2]
                pb0 = pby[rg % 2]
                P.mms(Ref(pb0, pb0.t[:, 0:256]),
                      [(rm16[8 * h + rg].full, Ref(C.xno[rg], C.xno[rg].t[:, cols]))])
                v_copy(P, yr.full, Ref(pb0, pb0.t[:, 0:256]))
                for i in range(4):
                    pb1 = pbz[zc[0] % 2]
                    zc[0] += 1
                    P.mms(Ref(pb1, pb1.t[:, 0:256]), [(rup16[(8 * h + rg) * 4 + i].full, yr.full)])
                    s_act(P, hact[h * 32 + rg * 4 + i].full, Ref(pb1, pb1.t[:, 0:256]), AF.Gelu)
    for mc in range(8):
        P.dma("g", cpm.full,
              Ref(C.W["w1_cpT"], C.W["w1_cpT"].t[128 * mc:128 * mc + 128, :]))
        for tg in range(8):
            cols = slice(256 * tg, 256 * tg + 256)
            P.mms(Ref(pb2, pb2.t[:, 0:256]),
                  [(Ref(cpm, cpm.t[:, 128 * kc:128 * kc + 128]), hact[kc].full)
                   for kc in range(128)])
            s_act(P, g16.full, Ref(pb2, pb2.t[:, 0:256]), AF.Silu)
            emit_update(P, C, mc, cols, g16.full)
    P.barrier()
    ph.close()


def emit_ffn_l2(P, C):
    import contextlib
    ph = contextlib.ExitStack()
    out = [_mkbuf(P, ph, f"f2_o{i}", (128, 2048), F32) for i in range(8)]
    o16 = [_mkbuf(P, ph, f"f2_o16{i}", (128, 512), BF16) for i in range(8)]
    cw1 = [_mkbuf(P, ph, f"f2_w1{i}", (128, 1024), BF16) for i in range(8)]
    cw2 = [_mkbuf(P, ph, f"f2_w2{i}", (128, 512), BF16) for i in range(8)]
    cw3 = [_mkbuf(P, ph, f"f2_w3{i}", (128, 256), BF16) for i in range(4)]
    cw4 = [_mkbuf(P, ph, f"f2_w4{i}", (128, 1), BF16) for i in range(2)]
    c1 = [_mkbuf(P, ph, f"f2_c1{i}", (128, 512), BF16) for i in range(8)]
    c2 = [_mkbuf(P, ph, f"f2_c2{i}", (128, 512), BF16) for i in range(4)]
    c3 = [_mkbuf(P, ph, f"f2_c3{i}", (128, 512), BF16) for i in range(2)]
    cb1 = _mkbuf(P, ph, "f2_cb1", (128, 8), F32)
    cb2 = _mkbuf(P, ph, "f2_cb2", (128, 4), F32)
    cb3 = _mkbuf(P, ph, "f2_cb3", (128, 2), F32)
    cb4 = _mkbuf(P, ph, "f2_cb4", (1, 1), F32)
    ar = _mkbuf(P, ph, "f2_ar", (1, 1), F32)
    srow = _mkbuf(P, ph, "f2_srow", (1, 4), F32)
    tfr = _mkbuf(P, ph, "f2_tfr", (1, 512), F32)
    cbt = _mkbuf(P, ph, "f2_cbt", (128, 512), F32)
    tmpt = _mkbuf(P, ph, "f2_tmp", (128, 512), F32)
    t1t = _mkbuf(P, ph, "f2_t1", (128, 512), F32)
    t2t = _mkbuf(P, ph, "f2_t2", (128, 512), F32)
    bmu = _mkbuf(P, ph, "f2_bmu", (128, 512), F32)
    brs = _mkbuf(P, ph, "f2_brs", (128, 512), F32)
    g16 = _mkbuf(P, ph, "f2_g16", (128, 512), BF16)
    for kc in range(8):
        P.dma("g", cw1[kc].full, Ref(C.W["w2_cw1"], C.W["w2_cw1"].t[128 * kc:128 * kc + 128, :]))
        P.dma("g", cw2[kc].full, Ref(C.W["w2_cw2"], C.W["w2_cw2"].t[128 * kc:128 * kc + 128, :]))
    for kc in range(4):
        P.dma("g", cw3[kc].full, Ref(C.W["w2_cw3"], C.W["w2_cw3"].t[128 * kc:128 * kc + 128, :]))
    for kc in range(2):
        P.dma("g", cw4[kc].full, Ref(C.W["w2_cw4"], C.W["w2_cw4"].t[128 * kc:128 * kc + 128, :]))
    P.dma("y", cb1.full, C.W["w2_cb1"].full)
    P.dma("y", cb2.full, C.W["w2_cb2"].full)
    P.dma("y", cb3.full, C.W["w2_cb3"].full)
    P.dma("y", cb4.full, C.W["w2_cb4"].full)
    P.dma("y", ar.full, C.W["w2_ar"].full)
    for tg in range(4):
        cols = slice(512 * tg, 512 * tg + 512)
        _row_bcasts(P, C, bmu, brs, tg)
        for c in range(8):
            P.dma("y", tmpt.full, C.x_src[c][:, cols])
            v_tt(P, t1t.full, tmpt.full, bmu.full, ALU.subtract)
            v_tt(P, Ref(out[c], out[c].t[:, cols]), t1t.full, brs.full, ALU.mult)
    v_ts(P, Ref(srow, srow.t[0:1, 0:1]), C.gf.full, 0.7, ALU.mult)
    v_ts(P, Ref(srow, srow.t[0:1, 0:1]), Ref(srow, srow.t[0:1, 0:1]), -0.5, ALU.add)
    v_ts(P, Ref(srow, srow.t[0:1, 1:2]), ar.full, 2.0, ALU.mult)
    pbAr = (C.pb[0], C.pb[3])
    pbB = C.pb[1]
    ac = [0]
    pairs = [((0, 1), (4, 5), ALU.add, ALU.subtract),
             ((2, 3), (6, 7), ALU.subtract, ALU.add)]
    for cyc in range(3):
        for tg in range(4):
            cols = slice(512 * tg, 512 * tg + 512)
            for c in range(8):
                v_copy(P, o16[c].full, Ref(out[c], out[c].t[:, cols]))
            for mc in range(8):
                pbA = pbAr[ac[0] % 2]
                ac[0] += 1
                P.mms(pbA.full, [(Ref(cw1[kc], cw1[kc].t[:, 128 * mc:128 * mc + 128]),
                                  o16[kc].full) for kc in range(8)])
                s_act(P, c1[mc].full, pbA.full, AF.Gelu, bias=Ref(cb1, cb1.t[:, mc:mc + 1]))
            for mc in range(4):
                pbA = pbAr[ac[0] % 2]
                ac[0] += 1
                P.mms(pbA.full, [(Ref(cw2[kc], cw2[kc].t[:, 128 * mc:128 * mc + 128]),
                                  c1[kc].full) for kc in range(8)])
                s_act(P, c2[mc].full, pbA.full, AF.Gelu, bias=Ref(cb2, cb2.t[:, mc:mc + 1]))
            for mc in range(2):
                pbA = pbAr[ac[0] % 2]
                ac[0] += 1
                P.mms(pbA.full, [(Ref(cw3[kc], cw3[kc].t[:, 128 * mc:128 * mc + 128]),
                                  c2[kc].full) for kc in range(4)])
                s_act(P, c3[mc].full, pbA.full, AF.Gelu, bias=Ref(cb3, cb3.t[:, mc:mc + 1]))
            P.mms(Ref(pbB, pbB.t[0:1, 0:512]),
                  [(Ref(cw4[kc], cw4[kc].t[:, 0:1]), c3[kc].full) for kc in range(2)])
            s_act(P, tfr.full, Ref(pbB, pbB.t[0:1, 0:512]), AF.Sigmoid,
                  bias=Ref(cb4, cb4.t[0:1, 0:1]))
            v_ts(P, tfr.full, tfr.full, 0.3, ALU.mult)
            v_ts(P, tfr.full, tfr.full, Ref(srow, srow.t[0:1, 0:1]), ALU.add)
            v_ts(P, tfr.full, tfr.full, Ref(srow, srow.t[0:1, 1:2]), ALU.mult)
            v_ts(P, tfr.full, tfr.full, 0.1, ALU.add)
            _bcast_row(P, C, cbt.full, tfr.full, C.pb[2], 512)
            for (A, Bq, sA, sB) in pairs:
                for j in range(2):
                    ca, cb_ = A[j], Bq[j]
                    v_copy(P, tmpt.full, Ref(out[ca], out[ca].t[:, cols]))
                    v_tt(P, t1t.full, cbt.full, Ref(out[cb_], out[cb_].t[:, cols]), ALU.mult)
                    v_tt(P, Ref(out[ca], out[ca].t[:, cols]),
                         Ref(out[ca], out[ca].t[:, cols]), t1t.full, sA)
                    v_tt(P, t2t.full, cbt.full, tmpt.full, ALU.mult)
                    v_tt(P, Ref(out[cb_], out[cb_].t[:, cols]),
                         Ref(out[cb_], out[cb_].t[:, cols]), t2t.full, sB)
    for tg in range(4):
        cols = slice(512 * tg, 512 * tg + 512)
        for c in range(8):
            s_act(P, g16.full, Ref(out[c], out[c].t[:, cols]), AF.Silu)
            emit_update(P, C, c, cols, g16.full)
    P.barrier()
    ph.close()



WNAMES = {
    0: ["w0_qT", "w0_kT", "w0_vT", "w0_bq", "w0_bk", "w0_bv", "w0_bvr",
        "w0_w1", "w0_w2", "w0_b1", "w0_b2r", "w0_gam", "w0_bet",
        "mt0_r0", "mt0_r1"],
    1: ["w1_qkvT", "w1_woT", "w1_rm", "w1_rup", "w1_cpT", "mt1_r0", "mt1_r1"],
    2: ["w2_qkvT", "w2_woT", "w2_bo", "w2_cw1", "w2_cw2", "w2_cw3", "w2_cw4",
        "w2_cb1", "w2_cb2", "w2_cb3", "w2_cb4", "w2_gw1", "w2_gw2", "w2_gw3",
        "w2_gb1", "w2_gb2", "w2_gb3", "w2_ar", "mt2_r0", "mt2_r1"],
    3: ["w3_q", "w3_k", "w3_bq", "w3_bk", "w3_bkr", "w3_rw", "w3_rbr",
        "w3_w1", "w3_w2", "w3_b1", "w3_b2"],
}
CONST_NAMES = ["c_ones_f", "c_eye_f", "c_bd64", "c_ones16", "c_zeros16",
               "c_eh", "c_cst"]


def _np2dt(a):
    return BF16 if a.dtype == BF else F32


def build_program(layers, prep0):
    nc = bass.Bass("TRN2", target_bir_lowering=False)
    P = Prog(nc)
    C = Ctx()
    C.W = {}
    innames = []

    x_in = P.dram("x_in", [1024, 2048], F32, kind="ExternalInput")
    innames.append("x_in")
    x_out = P.dram("x_out", [1024, 2048], F32, kind="ExternalOutput")
    xmid = [P.dram(f"x_mid{i}", [1024, 2048], F32) for i in range(len(layers) - 1)]

    for nm in CONST_NAMES:
        innames.append(nm)
    for l in layers:
        for nm in WNAMES[l]:
            a = prep0[nm]
            C.W[nm] = P.dram(nm, list(a.shape), _np2dt(a), kind="ExternalInput")
            innames.append(nm)

    _load_consts(P, C)

    C.pb = [P.psum(f"pb{i}") for i in range(8)]
    C.xno = [P.sbuf(f"xno{c}", [128, 2048], BF16) for c in range(8)]
    C.r_dram = P.dram("r_scr", [2, 2048], F32)
    C.rowscr = P.sbuf("rowscr", [1, 1024], F32)
    C.u_x = [P.sbuf(f"u_x{i}", [128, 512], F32) for i in range(2)]
    C.u_s = [P.sbuf(f"u_s{i}", [128, 512], F32) for i in range(2)]

    if 2 in layers:
        C.gf = P.sbuf("gf", [1, 1], F32)

    C.gin = P.dram("gin", [1024, 2048], BF16)
    C.gout = P.dram("gout", [2048, 2048], BF16)
    C.s_dram = P.dram("s_scr", [1032, 2048], F32)
    C.q_dram = P.dram("q_scr", [1024, 2048], BF16)
    C.o_dram = P.dram("o_scr", [1024, 2048], F32)

    for li, l in enumerate(layers):
        src = x_in if li == 0 else xmid[li - 1]
        dst = x_out if li == len(layers) - 1 else xmid[li]
        C.x_src = [Chunk(src, 128 * c) for c in range(8)]
        C.x_out = dst
        if l == 0:
            C.mtab = (C.W["mt0_r0"], C.W["mt0_r1"])
        elif l == 1:
            C.mtab = (C.W["mt1_r0"], C.W["mt1_r1"])
        elif l == 2:
            C.mtab = (C.W["mt2_r0"], C.W["mt2_r1"])
        else:
            C.mtab = None
        import contextlib
        lst = contextlib.ExitStack()
        if l == 0:
            C.recip_l0 = [_mkbuf(P, lst, f"recip0_{i}", (128, 512), F32)
                          for i in range(4)]
        emit_ln(P, C, l)
        emit_gather(P, C)
        if l == 0:
            emit_attn_l0(P, C)
        else:
            emit_attn_mh(P, C, l)
        if l in (0, 3):
            emit_moe(P, C, l)
        elif l == 1:
            emit_ffn_l1(P, C)
        else:
            emit_ffn_l2(P, C)
        P.barrier()
        lst.close()

    P.finish()
    P.build_block()
    return nc, innames


_prog_cache = {}


def _get_program(layers, prep0):
    key = tuple(layers)
    if key not in _prog_cache:
        _prog_cache[key] = build_program(layers, prep0)
    return _prog_cache[key]


def kernel(**inputs):
    percore = host_prep(inputs)
    xcur = [m["x_in"] for m in percore]
    stages = [[0, 1, 2, 3]]
    for layers in stages:
        nc, innames = _get_program(layers, percore[0])
        in_maps = []
        for c in range(NCORE):
            im = {n: percore[c][n] for n in innames if n != "x_in"}
            im["x_in"] = np.ascontiguousarray(xcur[c])
            in_maps.append(im)
        res = run_bass_kernel_spmd(nc, in_maps, core_ids=list(range(NCORE)))
        xcur = [np.asarray(res.results[c]["x_out"]) for c in range(NCORE)]
    return host_finish(xcur)


if __name__ == "__main__":
    import reference as _r
    inp = _r.setup_inputs()
    inp = {k: np.asarray(v) for k, v in inp.items()}
    out = kernel(**inp)
    ref = np.asarray(_r.reference(**{k: v for k, v in inp.items()}))
    num = float(np.mean((out - ref) ** 2))
    den = float(np.mean(ref ** 2))
    print("resid var ratio:", num / den)
